# version 05
# baseline (speedup 1.0000x reference)
import numpy as np
from contextlib import ExitStack

import concourse.bass as bass
import concourse.mybir as mybir
from concourse.bass_utils import run_bass_kernel_spmd

F32 = mybir.dt.float32
BF16 = mybir.dt.bfloat16
AF = mybir.ActivationFunctionType
OP = mybir.AluOpType
AX = mybir.AxisListType

N_CORES = 8
D = 1024
TC = 4096
TT = 1024
NTILE = TC // TT
NG = TT // 512
NE = 16
FI = 512
NSQ = 14
NPW = 6
SIGMA_STAGE = 99
DEBUG_SIGMA = False
EPS_A = 1e-4
LN_EPS = 1e-5


class Sem:
    def __init__(self, es, nc, name):
        self.h = es.enter_context(nc.semaphore(name))
        self.n = 0

    def inc(self, k=1):
        self.n += k
        return self.n


def emit_sigma(nc, block, es, wT, scale_out, consts, sems, tag):
    s_mm, s_ve, s_sc, s_ld = sems["mm"], sems["ve"], sems["sc"], sems["ld"]
    ones_col, ones_row, ones_bf = consts["ones_col"], consts["ones_row"], consts["ones_bf"]

    wt = es.enter_context(nc.sbuf_tensor(f"sg_wt{tag}", [128, 8, 1024], F32))
    Bf = es.enter_context(nc.sbuf_tensor(f"sg_Bf{tag}", [128, 8, 1024], F32))
    M0 = es.enter_context(nc.sbuf_tensor(f"sg_M0{tag}", [128, 8, 1024], BF16))
    M1 = es.enter_context(nc.sbuf_tensor(f"sg_M1{tag}", [128, 8, 1024], BF16))
    junk = es.enter_context(nc.sbuf_tensor(f"sg_junk{tag}", [128, 512], BF16))
    acc = es.enter_context(nc.sbuf_tensor(f"sg_acc{tag}", [128, 16], F32))
    r1 = es.enter_context(nc.sbuf_tensor(f"sg_r1{tag}", [128, 1], F32))
    inv1 = es.enter_context(nc.sbuf_tensor(f"sg_inv1{tag}", [128, 1], F32))
    ssc = es.enter_context(nc.sbuf_tensor(f"sg_ssc{tag}", [1, 1], F32))
    vv = es.enter_context(nc.sbuf_tensor(f"sg_vv{tag}", [128, 8], F32))
    tt = es.enter_context(nc.sbuf_tensor(f"sg_tt{tag}", [128, 8], F32))
    prod = es.enter_context(nc.sbuf_tensor(f"sg_prod{tag}", [128, 8], F32))
    den_sb = es.enter_context(nc.sbuf_tensor(f"sg_den{tag}", [128, 8], F32))
    n2 = es.enter_context(nc.sbuf_tensor(f"sg_n2{tag}", [1, 2], F32))
    ps0 = es.enter_context(nc.psum_tensor(f"sg_ps0{tag}", [128, 512], F32))
    ps1 = es.enter_context(nc.psum_tensor(f"sg_ps1{tag}", [128, 512], F32))
    pv = es.enter_context(nc.psum_tensor(f"sg_pv{tag}", [128, 1], F32))
    pv2 = es.enter_context(nc.psum_tensor(f"sg_pv2{tag}", [128, 1], F32))

    ld_gate = s_ld.n + 16
    ve_prev = s_ve.n
    sc_prev = s_sc.n
    mm_prev = s_mm.n

    def _(s):
        s.wait_ge(s_ve.h, ve_prev)
        s.wait_ge(s_sc.h, sc_prev)
        s.wait_ge(s_mm.h, mm_prev)
        s.dma_start(out=wt[:, :, :], in_=wT.rearrange("(c p) n -> p c n", p=128)).then_inc(s_ld.h, 16)
    block.sync(_)
    s_ld.inc(16)

    def sq_pass(src, dst, is_b_build):
        mm_base = s_mm.n
        ve_base = s_ve.n
        sc_base = s_sc.n

        def _t(t):
            if is_b_build:
                t.wait_ge(s_ld.h, ld_gate)
            t.wait_ge(s_ve.h, ve_base)
            t.wait_ge(s_sc.h, sc_base)
            for k in range(16):
                iblk, jg = k // 2, k % 2
                ps = ps0 if jg == 0 else ps1
                if k >= 2:
                    t.wait_ge(s_ve.h, ve_base + k - 1)
                    t.wait_ge(s_sc.h, sc_base + k - 1)
                for d in range(8):
                    r = t.matmul(ps[:, :], src[:, d, iblk * 128:(iblk + 1) * 128],
                                 src[:, d, jg * 512:(jg + 1) * 512],
                                 start=(d == 0), stop=(d == 7))
                r.then_inc(s_mm.h, 1)
        block.tensor(_t)
        s_mm.inc(16)

        def _v(v):
            for k in range(16):
                iblk, jg = k // 2, k % 2
                ps = ps0 if jg == 0 else ps1
                v.wait_ge(s_mm.h, mm_base + k + 1)
                tgt = Bf if is_b_build else dst
                r = v.tensor_copy(out=tgt[:, iblk, jg * 512:(jg + 1) * 512], in_=ps[:, :])
                r.then_inc(s_ve.h, 1)
        block.vector(_v)
        s_ve.inc(16)

        def _s(s):
            for k in range(16):
                iblk, jg = k // 2, k % 2
                s.wait_ge(s_ve.h, ve_base + k + 1)
                src_sb = (Bf if is_b_build else dst)[:, iblk, jg * 512:(jg + 1) * 512]
                if is_b_build:
                    r = s.activation(out=M0[:, iblk, jg * 512:(jg + 1) * 512], in_=src_sb,
                                     func=AF.Copy, bias=0.0)
                else:
                    r = s.activation(out=junk[:, :], in_=src_sb, func=AF.Abs,
                                     accum_out=acc[:, k:k + 1])
                r.then_inc(s_sc.h, 1)
        block.scalar(_s)
        s_sc.inc(16)

    sq_pass(wt, None, True)

    if SIGMA_STAGE < 2:
        return
    bufs = [M0, M1]
    for it in range(NSQ):
        src, dst = bufs[it % 2], bufs[(it + 1) % 2]
        sq_pass(src, dst, False)

        sc_gate = s_sc.n
        def _v2(v, sc_gate=sc_gate):
            v.wait_ge(s_sc.h, sc_gate)
            r = v.tensor_reduce(out=r1[:, :], in_=acc[:, :], axis=AX.X, op=OP.add)
            r.then_inc(s_ve.h, 1)
        block.vector(_v2)
        s_ve.inc(1)
        ve_g = s_ve.n
        def _t2(t, ve_g=ve_g):
            t.wait_ge(s_ve.h, ve_g)
            r = t.matmul(pv[:1, :], ones_col[:, :], r1[:, :], start=True, stop=True)
            r.then_inc(s_mm.h, 1)
        block.tensor(_t2)
        s_mm.inc(1)
        mm_g = s_mm.n
        def _v3(v, mm_g=mm_g):
            v.wait_ge(s_mm.h, mm_g)
            r = v.tensor_copy(out=ssc[:, :], in_=pv[:1, :])
            r.then_inc(s_ve.h, 1)
        block.vector(_v3)
        s_ve.inc(1)
        ve_g2 = s_ve.n
        def _t3(t, ve_g2=ve_g2):
            t.wait_ge(s_ve.h, ve_g2)
            r = t.matmul(pv2[:, :], ones_row[:1, :], ssc[:1, :], start=True, stop=True)
            r.then_inc(s_mm.h, 1)
        block.tensor(_t3)
        s_mm.inc(1)
        mm_g2 = s_mm.n
        def _v4(v, dst=dst, mm_g2=mm_g2):
            v.wait_ge(s_mm.h, mm_g2)
            v.reciprocal(out=inv1[:, :], in_=pv2[:, :])
            v.drain()
            for c in range(8):
                r = v.tensor_scalar(out=dst[:, c, :], in0=dst[:, c, :],
                                    scalar1=inv1[:, :], scalar2=256.0,
                                    op0=OP.mult, op1=OP.mult)
            r.then_inc(s_ve.h, 1)
        block.vector(_v4)
        s_ve.inc(1)

    Mfin = bufs[NSQ % 2]
    if SIGMA_STAGE < 3:
        return

    for pblk in range(8):
        gate = s_ve.n
        def _t(t, pblk=pblk, gate=gate):
            t.wait_ge(s_ve.h, gate)
            for q in range(8):
                r = t.matmul(pv[:, :], Mfin[:, q, pblk * 128:(pblk + 1) * 128],
                             ones_bf[:, :], start=(q == 0), stop=(q == 7))
            r.then_inc(s_mm.h, 1)
        block.tensor(_t)
        s_mm.inc(1)
        mm_g = s_mm.n
        def _v(v, pblk=pblk, mm_g=mm_g):
            v.wait_ge(s_mm.h, mm_g)
            r = v.tensor_copy(out=vv[:, pblk:pblk + 1], in_=pv[:, :])
            r.then_inc(s_ve.h, 1)
        block.vector(_v)
        s_ve.inc(1)

    if SIGMA_STAGE < 4:
        return
    vbuf, tbuf = vv, tt
    for step in range(NPW + 1):
        for pblk in range(8):
            gate = s_ve.n
            def _t(t, pblk=pblk, vbuf=vbuf, gate=gate):
                t.wait_ge(s_ve.h, gate)
                for q in range(8):
                    r = t.matmul(pv[:, :], Bf[:, q, pblk * 128:(pblk + 1) * 128],
                                 vbuf[:, q:q + 1], start=(q == 0), stop=(q == 7))
                r.then_inc(s_mm.h, 1)
            block.tensor(_t)
            s_mm.inc(1)
            mm_g = s_mm.n
            def _v(v, pblk=pblk, tbuf=tbuf, mm_g=mm_g):
                v.wait_ge(s_mm.h, mm_g)
                r = v.tensor_copy(out=tbuf[:, pblk:pblk + 1], in_=pv[:, :])
                r.then_inc(s_ve.h, 1)
            block.vector(_v)
            s_ve.inc(1)
        vbuf, tbuf = tbuf, vbuf
    vlast, tlast = tbuf, vbuf

    if SIGMA_STAGE < 5:
        return
    def _v5(v):
        v.tensor_tensor(out=prod[:, :], in0=vlast[:, :], in1=tlast[:, :], op=OP.mult)
        r = v.tensor_tensor(out=den_sb[:, :], in0=vlast[:, :], in1=vlast[:, :], op=OP.mult)
        r.then_inc(s_ve.h, 1)
    block.vector(_v5)
    s_ve.inc(1)

    ve_g = s_ve.n
    def _t4(t, ve_g=ve_g):
        t.wait_ge(s_ve.h, ve_g)
        for q in range(8):
            t.matmul(pv[:1, :], ones_col[:, :], prod[:, q:q + 1], start=(q == 0), stop=(q == 7))
        for q in range(8):
            r = t.matmul(pv2[:1, :], ones_col[:, :], den_sb[:, q:q + 1], start=(q == 0), stop=(q == 7))
        r.then_inc(s_mm.h, 1)
    block.tensor(_t4)
    s_mm.inc(1)

    mm_g = s_mm.n
    def _v6(v, mm_g=mm_g):
        v.wait_ge(s_mm.h, mm_g)
        v.tensor_copy(out=n2[:, 0:1], in_=pv[:1, :])
        v.tensor_copy(out=n2[:, 1:2], in_=pv2[:1, :])
        v.drain()
        v.reciprocal(out=n2[:, 1:2], in_=n2[:, 1:2])
        v.drain()
        r = v.tensor_tensor(out=scale_out[:, :], in0=n2[:, 0:1], in1=n2[:, 1:2], op=OP.mult)
        r.then_inc(s_ve.h, 1)
    block.vector(_v6)
    s_ve.inc(1)

    ve_g = s_ve.n
    def _s2(s, ve_g=ve_g):
        s.wait_ge(s_ve.h, ve_g)
        r = s.activation(out=scale_out[:, :], in_=scale_out[:, :], func=AF.Sqrt)
        r.then_inc(s_sc.h, 1)
    block.scalar(_s2)
    s_sc.inc(1)
    sc_g = s_sc.n
    def _v7(v, sc_g=sc_g):
        v.wait_ge(s_sc.h, sc_g)
        r = v.tensor_scalar(out=scale_out[:, :], in0=scale_out[:, :], scalar1=1.0,
                            scalar2=None, op0=OP.max)
        r.then_inc(s_ve.h, 1)
    block.vector(_v7)
    s_ve.inc(1)


def build_program():
    nc = bass.Bass("TRN2", target_bir_lowering=False, detect_race_conditions=False)

    xT = nc.dram_tensor("xT", [D, TC], F32, kind="ExternalInput").ap()
    outT = nc.dram_tensor("outT", [D, TC], F32, kind="ExternalOutput").ap()
    w_inT = nc.dram_tensor("w_inT", [D, D], F32, kind="ExternalInput").ap()
    w_outT = nc.dram_tensor("w_outT", [D, D], F32, kind="ExternalInput").ap()
    wpT = nc.dram_tensor("wpT", [D, 3 * D], F32, kind="ExternalInput").ap()
    gate_wT = nc.dram_tensor("gate_wT", [D, NE], F32, kind="ExternalInput").ap()
    gate_b = nc.dram_tensor("gate_b", [NE, 1], F32, kind="ExternalInput").ap()
    w13T = nc.dram_tensor("w13T", [NE, D, 2 * FI], F32, kind="ExternalInput").ap()
    w2T = nc.dram_tensor("w2T", [NE, FI, D], F32, kind="ExternalInput").ap()
    w13sT = nc.dram_tensor("w13sT", [D, 2 * FI], F32, kind="ExternalInput").ap()
    w2sT = nc.dram_tensor("w2sT", [FI, D], F32, kind="ExternalInput").ap()
    alpha_log = nc.dram_tensor("alpha_log", [128, 8], F32, kind="ExternalInput").ap()
    delta_bias = nc.dram_tensor("delta_bias", [128, 8], F32, kind="ExternalInput").ap()
    bp = nc.dram_tensor("bp", [128, 24], F32, kind="ExternalInput").ap()
    log_gamma = nc.dram_tensor("log_gamma", [1, 1], F32, kind="ExternalInput").ap()
    eye = nc.dram_tensor("eye", [128, 128], F32, kind="ExternalInput").ap()
    sel_e = nc.dram_tensor("sel_e", [NE, NE, 128], F32, kind="ExternalInput").ap()

    es0 = ExitStack()
    eye_sb = es0.enter_context(nc.sbuf_tensor("eye_sb", [128, 128], F32))
    ones_col = es0.enter_context(nc.sbuf_tensor("ones_col", [128, 1], F32))
    ones_row = es0.enter_context(nc.sbuf_tensor("ones_row", [1, 128], F32))
    ones_bf = es0.enter_context(nc.sbuf_tensor("ones_bf", [128, 1], BF16))
    scale1 = es0.enter_context(nc.sbuf_tensor("scale1", [1, 1], F32))
    scale2 = es0.enter_context(nc.sbuf_tensor("scale2", [1, 1], F32))
    attn_scale = es0.enter_context(nc.sbuf_tensor("attn_scale", [128, 1], F32))
    lg_sb = es0.enter_context(nc.sbuf_tensor("lg_sb", [1, 1], F32))
    alpha_n = es0.enter_context(nc.sbuf_tensor("alpha_n", [128, 8], F32))
    db_comb = es0.enter_context(nc.sbuf_tensor("db_comb", [128, 8], F32))
    bp_b = es0.enter_context(nc.sbuf_tensor("bp_b", [128, 8], F32))
    bp_c = es0.enter_context(nc.sbuf_tensor("bp_c", [128, 8], F32))
    gate_b_sb = es0.enter_context(nc.sbuf_tensor("gate_b_sb", [NE, 1], F32))
    gb_bcast = es0.enter_context(nc.sbuf_tensor("gb_bcast", [128, NE], F32))
    sel_sb = es0.enter_context(nc.sbuf_tensor("sel_sb", [NE, NE, 128], F32))
    gwt = es0.enter_context(nc.sbuf_tensor("gwt", [128, 8, NE], F32))
    carry = es0.enter_context(nc.sbuf_tensor("carry", [128, 8], F32))
    tmp8 = es0.enter_context(nc.sbuf_tensor("tmp8", [128, 8], F32))
    cpool = es0.enter_context(nc.sbuf_tensor("cpool", [128, 2], F32))
    Mcol = es0.enter_context(nc.sbuf_tensor("Mcol", [128, 1], F32))
    negM = es0.enter_context(nc.sbuf_tensor("negM", [128, 1], F32))

    sems = {}
    for name in ["ld", "mm", "ve", "sc", "xin", "xout", "wa", "we0", "we1"]:
        sems[name] = Sem(es0, nc, "s_" + name)
    s_ld, s_mm, s_ve, s_sc = sems["ld"], sems["mm"], sems["ve"], sems["sc"]
    s_xin, s_xout, s_wa = sems["xin"], sems["xout"], sems["wa"]
    s_we = [sems["we0"], sems["we1"]]

    consts = {"ones_col": ones_col, "ones_row": ones_row, "ones_bf": ones_bf}

    with ExitStack() as esA:
        block = esA.enter_context(nc.Block())

        def _(s):
            s.dma_start(out=eye_sb[:, :], in_=eye).then_inc(s_ld.h, 16)
            s.dma_start(out=lg_sb[:, :], in_=log_gamma).then_inc(s_ld.h, 16)
            s.dma_start(out=alpha_n[:, :], in_=alpha_log).then_inc(s_ld.h, 16)
            s.dma_start(out=db_comb[:, :], in_=delta_bias).then_inc(s_ld.h, 16)
            s.dma_start(out=tmp8[:, :], in_=bp[:, 0:8]).then_inc(s_ld.h, 16)
            s.dma_start(out=bp_b[:, :], in_=bp[:, 8:16]).then_inc(s_ld.h, 16)
            s.dma_start(out=bp_c[:, :], in_=bp[:, 16:24]).then_inc(s_ld.h, 16)
            s.dma_start(out=gate_b_sb[:, :], in_=gate_b).then_inc(s_ld.h, 16)
            s.dma_start(out=gwt[:, :, :], in_=gate_wT.rearrange("(c p) e -> p c e", p=128)).then_inc(s_ld.h, 16)
            s.dma_start(out=sel_sb[:, :, :], in_=sel_e.rearrange("e k j -> k e j")).then_inc(s_ld.h, 16)
        block.sync(_)
        s_ld.inc(16 * 10)

        ld_g = s_ld.n
        def _(v, ld_g=ld_g):
            v.wait_ge(s_ld.h, ld_g)
            v.memset(ones_col[:, :], 1.0)
            v.memset(ones_row[:, :], 1.0)
            v.memset(ones_bf[:, :], 1.0)
            v.memset(carry[:, :], 0.0)
            v.memset(cpool[:, 0:1], LN_EPS)
            v.memset(cpool[:, 1:2], 1e-12)
            v.memset(Mcol[:, :], 1.0)
            v.drain()
            v.tensor_scalar(out=Mcol[:, :], in0=Mcol[:, :], scalar1=float(2.0 ** -29),
                            scalar2=1.0, op0=OP.add, op1=OP.subtract)
            v.drain()
            v.tensor_scalar(out=Mcol[:, :], in0=Mcol[:, :], scalar1=float(2.0 ** 58),
                            scalar2=None, op0=OP.mult)
            v.drain()
            v.tensor_scalar(out=negM[:, :], in0=Mcol[:, :], scalar1=-1.0, scalar2=None, op0=OP.mult)
            v.drain()
            r = v.tensor_tensor(out=db_comb[:, :], in0=db_comb[:, :], in1=tmp8[:, :], op=OP.add)
            r.then_inc(s_ve.h, 1)
        block.vector(_)
        s_ve.inc(1)
        def _(s, ld_g=ld_g):
            s.wait_ge(s_ld.h, ld_g)
            s.activation(out=alpha_n[:, :], in_=alpha_n[:, :], func=AF.Exp)
            s.add(alpha_n[:, :], alpha_n[:, :], 1.0)
            s.activation(out=alpha_n[:, :], in_=alpha_n[:, :], func=AF.Ln)
            r = s.mul(alpha_n[:, :], alpha_n[:, :], -1.0)
            r.then_inc(s_sc.h, 1)
        block.scalar(_)
        s_sc.inc(1)

        with ExitStack() as esS:
            emit_sigma(nc, block, esS, w_inT, scale1, consts, sems, "a")
        with ExitStack() as esS:
            emit_sigma(nc, block, esS, w_outT, scale2, consts, sems, "b")

        with ExitStack() as esS:
            pvb = esS.enter_context(nc.psum_tensor("pvb", [128, 1], F32))
            n2gb = esS.enter_context(nc.sbuf_tensor("n2gb", [1, NE], F32))
            ve_g = s_ve.n
            def _(s, ve_g=ve_g):
                s.wait_ge(s_ve.h, ve_g)
                r = s.activation(out=lg_sb[:, :], in_=lg_sb[:, :], func=AF.Exp)
                r.then_inc(s_sc.h, 1)
            block.scalar(_)
            s_sc.inc(1)
            sc_g = s_sc.n
            def _(v, sc_g=sc_g):
                v.wait_ge(s_sc.h, sc_g)
                v.tensor_tensor(out=scale1[:, :], in0=scale1[:, :], in1=scale2[:, :], op=OP.mult)
                v.drain()
                v.reciprocal(out=scale1[:, :], in_=scale1[:, :])
                v.drain()
                r = v.tensor_tensor(out=scale1[:, :], in0=scale1[:, :], in1=lg_sb[:, :], op=OP.mult)
                r.then_inc(s_ve.h, 1)
            block.vector(_)
            s_ve.inc(1)
            ve_g = s_ve.n
            def _(t, ve_g=ve_g):
                t.wait_ge(s_ve.h, ve_g)
                r = t.matmul(pvb[:, :], ones_row[:1, :], scale1[:1, :], start=True, stop=True)
                r.then_inc(s_mm.h, 1)
            block.tensor(_)
            s_mm.inc(1)
            mm_g = s_mm.n
            def _(v, mm_g=mm_g):
                v.wait_ge(s_mm.h, mm_g)
                r = v.tensor_copy(out=attn_scale[:, :], in_=pvb[:, :])
                r.then_inc(s_ve.h, 1)
            block.vector(_)
            s_ve.inc(1)
            pgb = esS.enter_context(nc.psum_tensor("pgb", [128, NE], F32))
            ve_g = s_ve.n
            def _(t, ve_g=ve_g):
                t.wait_ge(s_ve.h, ve_g)
                r = t.transpose(pgb[:1, :NE], gate_b_sb[:, :], eye_sb[:NE, :NE])
                r.then_inc(s_mm.h, 1)
            block.tensor(_)
            s_mm.inc(1)
            mm_g = s_mm.n
            def _(v, mm_g=mm_g):
                v.wait_ge(s_mm.h, mm_g)
                r = v.tensor_copy(out=lg_sb[:, :] if False else n2gb[:1, :], in_=pgb[:1, :NE])
                r.then_inc(s_ve.h, 1)
            block.vector(_)
            s_ve.inc(1)
            ve_g = s_ve.n
            def _(t, ve_g=ve_g):
                t.wait_ge(s_ve.h, ve_g)
                r = t.matmul(pgb[:, :], ones_row[:1, :], n2gb[:1, :], start=True, stop=True)
                r.then_inc(s_mm.h, 1)
            block.tensor(_)
            s_mm.inc(1)
            mm_g = s_mm.n
            def _(v, mm_g=mm_g):
                v.wait_ge(s_mm.h, mm_g)
                r = v.tensor_copy(out=gb_bcast[:, :], in_=pgb[:, :])
                r.then_inc(s_ve.h, 1)
            block.vector(_)
            s_ve.inc(1)

    if DEBUG_SIGMA:
        with ExitStack() as esD:
            block = esD.enter_context(nc.Block())
            sD = Sem(esD, nc, "s_dbg")
            def _(g):
                g.dma_start(out=outT[0:1, 0:1], in_=attn_scale[0:1, :]).then_inc(sD.h, 16)
                g.dma_start(out=outT[0:1, 4:5], in_=scale1[:, :]).then_inc(sD.h, 16)
                g.dma_start(out=outT[0:1, 8:9], in_=scale2[:, :]).then_inc(sD.h, 16)
                g.wait_ge(sD.h, 48)
            block.gpsimd(_)

    with ExitStack() as esB:
        x_sb = esB.enter_context(nc.sbuf_tensor("x_sb", [128, 8, TT], F32))
        xn_sb = esB.enter_context(nc.sbuf_tensor("xn_sb", [128, 8, TT], F32))
        Ra = esB.enter_context(nc.sbuf_tensor("Ra", [128, 8, TT], F32))
        Rb = esB.enter_context(nc.sbuf_tensor("Rb", [128, 8, TT], F32))
        Rc = esB.enter_context(nc.sbuf_tensor("Rc", [128, 8, TT], F32))
        ring = esB.enter_context(nc.sbuf_tensor("ring", [128, 8, 512], F32))
        scr = esB.enter_context(nc.sbuf_tensor("scr", [128, 4096], F32))
        comb_em = esB.enter_context(nc.sbuf_tensor("comb_em", [NE, TT], F32))
        btok = esB.enter_context(nc.sbuf_tensor("btok", [128, 8, NE], F32))
        stok = esB.enter_context(nc.sbuf_tensor("stok", [128, 8, NE], F32))
        rtok = esB.enter_context(nc.sbuf_tensor("rtok", [128, 8, NE], F32))
        thr8 = esB.enter_context(nc.sbuf_tensor("thr8", [128, 8], F32))
        invd8 = esB.enter_context(nc.sbuf_tensor("invd8", [128, 8], F32))

        P = [esB.enter_context(nc.psum_tensor(f"P{i}", [128, 512], F32)) for i in range(8)]
        stats = [Rc[:1, 0, 0:512], Rc[:1, 0, 512:1024], Rc[:1, 1, 0:512],
                 Rc[:1, 1, 512:1024], Rc[:1, 2, 0:512]]
        scores_em = Rc[0:NE, 3, :]

        block = esB.enter_context(nc.Block())

        w13_slot = [Ra, Rb]

        def w2_slot(slot):
            return Rc[:, 4 * slot:4 * slot + 4, :]

        for j in range(NTILE):
            t0, t1 = j * TT, (j + 1) * TT

            xo_gate = s_xout.n
            def _(g, t0=t0, t1=t1, xo_gate=xo_gate):
                g.wait_ge(s_xout.h, xo_gate)
                g.dma_start(out=x_sb[:, :, :],
                            in_=xT[:, t0:t1].rearrange("(c p) t -> p c t", p=128)).then_inc(s_xin.h, 16)
            block.gpsimd(_)
            s_xin.inc(16)

            panels = ([("wp", wpT[:, p * 256:(p + 1) * 256]) for p in range(12)]
                      + [("win", w_inT[:, p * 256:(p + 1) * 256]) for p in range(4)]
                      + [("wout", w_outT[:, p * 256:(p + 1) * 256]) for p in range(4)])
            wa_base = s_wa.n

            def emit_panel_load(pi, gate_mm, gate_ve):
                def _(s, pi=pi, gate_mm=gate_mm, gate_ve=gate_ve):
                    if gate_mm is not None:
                        s.wait_ge(s_mm.h, gate_mm)
                    if gate_ve is not None:
                        s.wait_ge(s_ve.h, gate_ve)
                    slot = pi % 2
                    s.dma_start(out=ring[:, :, slot * 256:(slot + 1) * 256],
                                in_=panels[pi][1].rearrange("(c p) n -> p c n", p=128)).then_inc(s_wa.h, 16)
                block.sync(_)
                s_wa.inc(16)

            emit_panel_load(0, s_mm.n, s_ve.n)
            emit_panel_load(1, s_mm.n, s_ve.n)
            pi_loaded = 2

            xin_gate = s_xin.n
            for g in range(NG):
                sc_pre = s_sc.n
                mm_pre = s_mm.n
                ve_pre = s_ve.n
                def _(s, g=g, xin_gate=xin_gate, mm_pre=mm_pre):
                    s.wait_ge(s_xin.h, xin_gate)
                    if g > 0:
                        s.wait_ge(s_mm.h, mm_pre)
                    for c in range(8):
                        r = s.activation(out=scr[:, c * 512:(c + 1) * 512],
                                         in_=x_sb[:, c, g * 512:(g + 1) * 512], func=AF.Square)
                    r.then_inc(s_sc.h, 1)
                block.scalar(_)
                s_sc.inc(1)
                sc_sq = s_sc.n
                def _(t, g=g, xin_gate=xin_gate, sc_sq=sc_sq, ve_pre=ve_pre):
                    t.wait_ge(s_xin.h, xin_gate)
                    t.wait_ge(s_sc.h, sc_sq)
                    t.wait_ge(s_ve.h, ve_pre)
                    for c in range(8):
                        t.matmul(P[0][:1, :], ones_col[:, :], x_sb[:, c, g * 512:(g + 1) * 512],
                                 start=(c == 0), stop=(c == 7))
                    for c in range(8):
                        r = t.matmul(P[1][:1, :], ones_col[:, :], scr[:, c * 512:(c + 1) * 512],
                                     start=(c == 0), stop=(c == 7))
                    r.then_inc(s_mm.h, 1)
                block.tensor(_)
                s_mm.inc(1)
                mm_g = s_mm.n
                def _(v, mm_g=mm_g):
                    v.wait_ge(s_mm.h, mm_g)
                    v.tensor_scalar(out=stats[0], in0=P[0][:1, :], scalar1=1.0 / D,
                                    scalar2=None, op0=OP.mult)
                    v.tensor_scalar(out=stats[1], in0=P[1][:1, :], scalar1=1.0 / D,
                                    scalar2=None, op0=OP.mult)
                    v.drain()
                    v.tensor_tensor(out=stats[2], in0=stats[0], in1=stats[0], op=OP.mult)
                    v.drain()
                    r = v.tensor_tensor(out=stats[1], in0=stats[1], in1=stats[2], op=OP.subtract)
                    r.then_inc(s_ve.h, 1)
                block.vector(_)
                s_ve.inc(1)
                ve_g = s_ve.n
                def _(s, ve_g=ve_g):
                    s.wait_ge(s_ve.h, ve_g)
                    r = s.activation(out=stats[2], in_=stats[1], func=AF.Sqrt, bias=cpool[:1, 0:1])
                    r.then_inc(s_sc.h, 1)
                block.scalar(_)
                s_sc.inc(1)
                sc_g = s_sc.n
                def _(v, sc_g=sc_g):
                    v.wait_ge(s_sc.h, sc_g)
                    v.reciprocal(out=stats[3], in_=stats[2])
                    v.drain()
                    r = v.tensor_tensor(out=stats[4], in0=stats[0], in1=stats[3], op=OP.mult)
                    r.then_inc(s_ve.h, 1)
                block.vector(_)
                s_ve.inc(1)
                ve_g = s_ve.n
                def _(t, ve_g=ve_g):
                    t.wait_ge(s_ve.h, ve_g)
                    t.matmul(P[2][:, :], ones_row[:1, :], stats[3], start=True, stop=True)
                    r = t.matmul(P[3][:, :], ones_row[:1, :], stats[4], start=True, stop=True)
                    r.then_inc(s_mm.h, 1)
                block.tensor(_)
                s_mm.inc(1)
                mm_g = s_mm.n
                def _(v, g=g, mm_g=mm_g):
                    v.wait_ge(s_mm.h, mm_g)
                    for c in range(8):
                        v.tensor_tensor(out=xn_sb[:, c, g * 512:(g + 1) * 512],
                                        in0=x_sb[:, c, g * 512:(g + 1) * 512], in1=P[2][:, :], op=OP.mult)
                        r = v.tensor_tensor(out=xn_sb[:, c, g * 512:(g + 1) * 512],
                                            in0=xn_sb[:, c, g * 512:(g + 1) * 512], in1=P[3][:, :], op=OP.subtract)
                    r.then_inc(s_ve.h, 1)
                block.vector(_)
                s_ve.inc(1)
            xn_ready = s_ve.n

            for g in range(NG):
                mm_pre_rnd = s_mm.n
                def _(v, g=g, mm_pre_rnd=mm_pre_rnd):
                    if g == 0:
                        v.wait_ge(s_ve.h, xn_ready)
                    else:
                        v.wait_ge(s_mm.h, mm_pre_rnd)
                    for c in range(8):
                        v.scalar_tensor_tensor(out=scr[:, c * 512:(c + 1) * 512],
                                               in0=xn_sb[:, c, g * 512:(g + 1) * 512],
                                               scalar=Mcol[:, :],
                                               in1=xn_sb[:, c, g * 512:(g + 1) * 512],
                                               op0=OP.mult, op1=OP.add)
                    for c in range(8):
                        r = v.scalar_tensor_tensor(out=scr[:, c * 512:(c + 1) * 512],
                                                   in0=xn_sb[:, c, g * 512:(g + 1) * 512],
                                                   scalar=negM[:, :],
                                                   in1=scr[:, c * 512:(c + 1) * 512],
                                                   op0=OP.mult, op1=OP.add)
                    r.then_inc(s_ve.h, 1)
                block.vector(_)
                s_ve.inc(1)
                rnd_ready = s_ve.n
                sc_pre = s_sc.n
                def _(t, g=g, sc_pre=sc_pre, rnd_ready=rnd_ready):
                    t.wait_ge(s_ve.h, rnd_ready)
                    t.wait_ge(s_sc.h, sc_pre)
                    for c in range(8):
                        r = t.matmul(P[4][:NE, :], gwt[:, c, :], scr[:, c * 512:(c + 1) * 512],
                                     start=(c == 0), stop=(c == 7))
                    r.then_inc(s_mm.h, 1)
                block.tensor(_)
                s_mm.inc(1)
                mm_g = s_mm.n
                def _(s, g=g, mm_g=mm_g):
                    s.wait_ge(s_mm.h, mm_g)
                    r = s.activation(out=scores_em[:, g * 512:(g + 1) * 512], in_=P[4][:NE, :], func=AF.Sigmoid)
                    r.then_inc(s_sc.h, 1)
                block.scalar(_)
                s_sc.inc(1)
            sc_scores = s_sc.n
            for sub in range(8):
                sc_g = s_sc.n
                def _(t, sub=sub, sc_g=sc_g):
                    t.wait_ge(s_sc.h, sc_scores)
                    if sub > 0:
                        t.wait_ge(s_sc.h, sc_g)
                    r = t.transpose(P[6][:, :NE], scores_em[:, sub * 128:(sub + 1) * 128], eye_sb[:NE, :NE])
                    r.then_inc(s_mm.h, 1)
                block.tensor(_)
                s_mm.inc(1)
                mm_g = s_mm.n
                def _(s, sub=sub, mm_g=mm_g):
                    s.wait_ge(s_mm.h, mm_g)
                    r = s.copy(out=stok[:, sub, :], in_=P[6][:, :NE])
                    r.then_inc(s_sc.h, 1)
                block.scalar(_)
                s_sc.inc(1)
            sc_routing = s_sc.n
            def _routing(v, sc_routing=sc_routing):
                v.wait_ge(s_sc.h, sc_routing)
                for sub in range(8):
                    v.tensor_tensor(out=btok[:, sub, :], in0=stok[:, sub, :], in1=gb_bcast[:, :], op=OP.add)
                for sub in range(8):
                    v.tensor_copy(out=rtok[:, sub, :], in_=btok[:, sub, :])
                for rnd in range(4):
                    for sub in range(8):
                        v.tensor_reduce(out=thr8[:, sub:sub + 1], in_=rtok[:, sub, :], axis=AX.X, op=OP.max)
                    if rnd < 3:
                        for sub in range(8):
                            v.tensor_scalar(out=scr[:, sub * 16:(sub + 1) * 16], in0=rtok[:, sub, :],
                                            scalar1=thr8[:, sub:sub + 1], scalar2=None, op0=OP.is_ge)
                        for sub in range(8):
                            v.scalar_tensor_tensor(out=rtok[:, sub, :], in0=scr[:, sub * 16:(sub + 1) * 16],
                                                   scalar=-1e30, in1=rtok[:, sub, :], op0=OP.mult, op1=OP.add)
                for sub in range(8):
                    v.tensor_scalar(out=scr[:, sub * 16:(sub + 1) * 16], in0=btok[:, sub, :],
                                    scalar1=thr8[:, sub:sub + 1], scalar2=None, op0=OP.is_ge)
                for sub in range(8):
                    v.tensor_tensor(out=rtok[:, sub, :], in0=stok[:, sub, :],
                                    in1=scr[:, sub * 16:(sub + 1) * 16], op=OP.mult)
                for sub in range(8):
                    v.tensor_reduce(out=invd8[:, sub:sub + 1], in_=rtok[:, sub, :], axis=AX.X, op=OP.add)
                v.drain()
                v.reciprocal(out=invd8[:, :], in_=invd8[:, :])
                v.drain()
                for sub in range(8):
                    v.tensor_scalar(out=btok[:, sub, :], in0=rtok[:, sub, :],
                                    scalar1=invd8[:, sub:sub + 1],
                                    scalar2=2.5, op0=OP.mult, op1=OP.mult)
                v.drain()
                v.tensor_copy(out=rtok[:, :, :], in_=btok[:, :, :])
                v.drain()
                v.scalar_tensor_tensor(out=btok[:, :, :], in0=rtok[:, :, :], scalar=Mcol[:, :],
                                       in1=rtok[:, :, :], op0=OP.mult, op1=OP.add)
                v.drain()
                r = v.scalar_tensor_tensor(out=btok[:, :, :], in0=rtok[:, :, :], scalar=negM[:, :],
                                           in1=btok[:, :, :], op0=OP.mult, op1=OP.add)
                r.then_inc(s_ve.h, 1)
            block.vector(_routing)
            s_ve.inc(1)
            for sub in range(8):
                ve_g = s_ve.n
                sc_g = s_sc.n
                def _(t, sub=sub, ve_g=ve_g, sc_g=sc_g):
                    t.wait_ge(s_ve.h, ve_g)
                    if sub > 0:
                        t.wait_ge(s_sc.h, sc_g)
                    r = t.transpose(P[5][:NE, :128], btok[:, sub, :], eye_sb[:, :])
                    r.then_inc(s_mm.h, 1)
                block.tensor(_)
                s_mm.inc(1)
                mm_g = s_mm.n
                def _(s, sub=sub, mm_g=mm_g):
                    s.wait_ge(s_mm.h, mm_g)
                    r = s.copy(out=comb_em[:, sub * 128:(sub + 1) * 128], in_=P[5][:NE, :128])
                    r.then_inc(s_sc.h, 1)
                block.scalar(_)
                s_sc.inc(1)
            comb_ready_sc = s_sc.n

            for pi in range(12):
                wa_gate = wa_base + 16 * (pi + 1)
                for blk in range(2):
                    kblk = pi * 2 + blk
                    sect = kblk // 8
                    loc = kblk % 8
                    for g in range(NG):
                        pband = P[2 * (kblk % 2) + g]
                        ve_g = s_ve.n
                        sc_g = s_sc.n
                        def _(t, pi=pi, blk=blk, g=g, wa_gate=wa_gate, pband=pband,
                              ve_g=ve_g, sc_g=sc_g):
                            t.wait_ge(s_wa.h, wa_gate)
                            t.wait_ge(s_ve.h, ve_g)
                            t.wait_ge(s_sc.h, sc_g)
                            slot = pi % 2
                            for c in range(8):
                                r = t.matmul(pband[:, :],
                                             ring[:, c, slot * 256 + blk * 128: slot * 256 + (blk + 1) * 128],
                                             xn_sb[:, c, g * 512:(g + 1) * 512],
                                             start=(c == 0), stop=(c == 7))
                            r.then_inc(s_mm.h, 1)
                        block.tensor(_)
                        s_mm.inc(1)
                        mm_g = s_mm.n
                        if sect == 0:
                            def _(s, loc=loc, g=g, pband=pband, mm_g=mm_g):
                                s.wait_ge(s_mm.h, mm_g)
                                s.activation(out=scr[:, 0:512], in_=pband[:, :], func=AF.Exp,
                                             bias=db_comb[:, loc:loc + 1])
                                s.add(scr[:, 0:512], scr[:, 0:512], 1.0)
                                s.activation(out=scr[:, 0:512], in_=scr[:, 0:512], func=AF.Ln)
                                r = s.activation(out=Ra[:, loc, g * 512:(g + 1) * 512], in_=scr[:, 0:512],
                                                 func=AF.Exp, scale=alpha_n[:, loc:loc + 1])
                                r.then_inc(s_sc.h, 1)
                            block.scalar(_)
                            s_sc.inc(1)
                            sc_g = s_sc.n
                            def _(v, loc=loc, g=g, sc_g=sc_g):
                                v.wait_ge(s_sc.h, sc_g)
                                r = v.tensor_scalar(out=Ra[:, loc, g * 512:(g + 1) * 512],
                                                    in0=Ra[:, loc, g * 512:(g + 1) * 512],
                                                    scalar1=1.0 - EPS_A, scalar2=None, op0=OP.min)
                                r.then_inc(s_ve.h, 1)
                            block.vector(_)
                            s_ve.inc(1)
                        else:
                            dst = Rb if sect == 1 else Rc
                            bias_t = bp_b if sect == 1 else bp_c
                            def _(s, loc=loc, g=g, pband=pband, dst=dst, bias_t=bias_t, mm_g=mm_g):
                                s.wait_ge(s_mm.h, mm_g)
                                r = s.activation(out=dst[:, loc, g * 512:(g + 1) * 512], in_=pband[:, :],
                                                 func=AF.Tanh, bias=bias_t[:, loc:loc + 1])
                                r.then_inc(s_sc.h, 1)
                            block.scalar(_)
                            s_sc.inc(1)
                if pi_loaded < 20:
                    emit_panel_load(pi_loaded, s_mm.n, None)
                    pi_loaded += 1

            for c in range(8):
                for g in range(NG):
                    sl = slice(g * 512, (g + 1) * 512)
                    sc_g = s_sc.n
                    def _(v, c=c, sl=sl, sc_g=sc_g):
                        v.wait_ge(s_sc.h, sc_g)
                        a_, b_, c_ = Ra[:, c, sl], Rb[:, c, sl], Rc[:, c, sl]
                        p_, r_, q_, d_ = scr[:, 512:1024], scr[:, 1024:1536], scr[:, 1536:2048], scr[:, 2048:2560]
                        v.tensor_tensor(out=p_, in0=a_, in1=a_, op=OP.mult)
                        v.tensor_tensor(out=q_, in0=c_, in1=c_, op=OP.mult)
                        v.tensor_tensor(out=p_, in0=p_, in1=q_, op=OP.add)
                        v.tensor_tensor(out=r_, in0=b_, in1=b_, op=OP.mult)
                        v.tensor_tensor(out=q_, in0=a_, in1=b_, op=OP.mult)
                        v.tensor_tensor(out=d_, in0=p_, in1=r_, op=OP.subtract)
                        v.tensor_tensor(out=d_, in0=d_, in1=d_, op=OP.mult)
                        v.tensor_tensor(out=q_, in0=q_, in1=q_, op=OP.mult)
                        v.tensor_scalar(out=q_, in0=q_, scalar1=4.0, scalar2=None, op0=OP.mult)
                        v.tensor_tensor(out=d_, in0=d_, in1=q_, op=OP.add)
                        r2 = v.tensor_tensor(out=p_, in0=p_, in1=r_, op=OP.add)
                        r2.then_inc(s_ve.h, 1)
                    block.vector(_)
                    s_ve.inc(1)
                    ve_g = s_ve.n
                    def _(s, ve_g=ve_g):
                        s.wait_ge(s_ve.h, ve_g)
                        r2 = s.activation(out=scr[:, 2560:3072], in_=scr[:, 2048:2560], func=AF.Sqrt, bias=cpool[:, 1:2])
                        r2.then_inc(s_sc.h, 1)
                    block.scalar(_)
                    s_sc.inc(1)
                    sc_g = s_sc.n
                    def _(v, sc_g=sc_g):
                        v.wait_ge(s_sc.h, sc_g)
                        r2 = v.tensor_tensor(out=scr[:, 512:1024], in0=scr[:, 512:1024], in1=scr[:, 2560:3072], op=OP.add)
                        r2.then_inc(s_ve.h, 1)
                    block.vector(_)
                    s_ve.inc(1)
                    ve_g = s_ve.n
                    def _(s, ve_g=ve_g):
                        s.wait_ge(s_ve.h, ve_g)
                        r2 = s.activation(out=scr[:, 512:1024], in_=scr[:, 512:1024], func=AF.Sqrt, scale=0.5, bias=cpool[:, 1:2])
                        r2.then_inc(s_sc.h, 1)
                    block.scalar(_)
                    s_sc.inc(1)
                    sc_g = s_sc.n
                    def _(v, c=c, sl=sl, sc_g=sc_g):
                        v.wait_ge(s_sc.h, sc_g)
                        a_, b_, c_ = Ra[:, c, sl], Rb[:, c, sl], Rc[:, c, sl]
                        v.tensor_scalar(out=scr[:, 512:1024], in0=scr[:, 512:1024], scalar1=1.0, scalar2=None, op0=OP.max)
                        v.reciprocal(out=scr[:, 1024:1536], in_=scr[:, 512:1024])
                        v.tensor_tensor(out=a_, in0=a_, in1=scr[:, 1024:1536], op=OP.mult)
                        v.tensor_tensor(out=b_, in0=b_, in1=scr[:, 1024:1536], op=OP.mult)
                        r2 = v.tensor_tensor(out=c_, in0=c_, in1=scr[:, 1024:1536], op=OP.mult)
                        r2.then_inc(s_ve.h, 1)
                    block.vector(_)
                    s_ve.inc(1)

            for pp in range(4):
                pi = 12 + pp
                wa_gate = wa_base + 16 * (pi + 1)
                for blk in range(2):
                    nblk = pp * 2 + blk
                    for g in range(NG):
                        pband = P[4 + g]
                        ve_g = s_ve.n
                        sc_g = s_sc.n
                        def _(t, pi=pi, blk=blk, g=g, wa_gate=wa_gate, pband=pband, ve_g=ve_g, sc_g=sc_g):
                            t.wait_ge(s_wa.h, wa_gate)
                            t.wait_ge(s_ve.h, ve_g)
                            t.wait_ge(s_sc.h, sc_g)
                            slot = pi % 2
                            for c in range(8):
                                r = t.matmul(pband[:, :],
                                             ring[:, c, slot * 256 + blk * 128: slot * 256 + (blk + 1) * 128],
                                             xn_sb[:, c, g * 512:(g + 1) * 512],
                                             start=(c == 0), stop=(c == 7))
                            r.then_inc(s_mm.h, 1)
                        block.tensor(_)
                        s_mm.inc(1)
                        mm_g = s_mm.n
                        def _(v, nblk=nblk, g=g, pband=pband, mm_g=mm_g):
                            v.wait_ge(s_mm.h, mm_g)
                            r = v.tensor_tensor(out=Rb[:, nblk, g * 512:(g + 1) * 512],
                                                in0=Rb[:, nblk, g * 512:(g + 1) * 512],
                                                in1=pband[:, :], op=OP.mult)
                            r.then_inc(s_ve.h, 1)
                        block.vector(_)
                        s_ve.inc(1)
                if pi_loaded < 20:
                    emit_panel_load(pi_loaded, s_mm.n, None)
                    pi_loaded += 1

            for c in range(8):
                def _(v, c=c):
                    v.tensor_tensor_scan(out=scr[:, 0:TT], data0=Ra[:, c, :], data1=Rb[:, c, :],
                                         initial=carry[:, c:c + 1], op0=OP.mult, op1=OP.add)
                    v.tensor_tensor(out=Ra[:, c, 0:1], in0=Rc[:, c, 0:1], in1=carry[:, c:c + 1], op=OP.mult)
                    v.tensor_tensor(out=Ra[:, c, 1:], in0=Rc[:, c, 1:], in1=scr[:, 0:TT - 1], op=OP.mult)
                    r = v.tensor_copy(out=carry[:, c:c + 1], in_=scr[:, TT - 1:TT])
                    r.then_inc(s_ve.h, 1)
                block.vector(_)
                s_ve.inc(1)
            yhat_ready = s_ve.n

            for pp in range(4):
                pi = 16 + pp
                wa_gate = wa_base + 16 * (pi + 1)
                for blk in range(2):
                    oblk = pp * 2 + blk
                    for g in range(NG):
                        pband = P[4 + g]
                        ve_g = s_ve.n
                        def _(t, pi=pi, blk=blk, g=g, wa_gate=wa_gate, pband=pband, ve_g=ve_g):
                            t.wait_ge(s_wa.h, wa_gate)
                            t.wait_ge(s_ve.h, ve_g)
                            slot = pi % 2
                            for c in range(8):
                                r = t.matmul(pband[:, :],
                                             ring[:, c, slot * 256 + blk * 128: slot * 256 + (blk + 1) * 128],
                                             Ra[:, c, g * 512:(g + 1) * 512],
                                             start=(c == 0), stop=(c == 7))
                            r.then_inc(s_mm.h, 1)
                        block.tensor(_)
                        s_mm.inc(1)
                        mm_g = s_mm.n
                        def _(v, oblk=oblk, g=g, pband=pband, mm_g=mm_g):
                            v.wait_ge(s_mm.h, mm_g)
                            r = v.scalar_tensor_tensor(out=x_sb[:, oblk, g * 512:(g + 1) * 512],
                                                       in0=pband[:, :], scalar=attn_scale[:, :],
                                                       in1=x_sb[:, oblk, g * 512:(g + 1) * 512],
                                                       op0=OP.mult, op1=OP.add)
                            r.then_inc(s_ve.h, 1)
                        block.vector(_)
                        s_ve.inc(1)
                if pi_loaded < 20:
                    emit_panel_load(pi_loaded, s_mm.n, None)
                    pi_loaded += 1
            attn_done_mm = s_mm.n
            attn_done_ve = s_ve.n

            slot_gate_mm = [attn_done_mm, attn_done_mm]

            def emit_expert_load(e):
                slot = e % 2
                gate = slot_gate_mm[slot]
                def _(s, e=e, slot=slot, gate=gate):
                    s.wait_ge(s_mm.h, gate)
                    s.wait_ge(s_ve.h, attn_done_ve)
                    if e < NE:
                        s.dma_start(out=w13_slot[slot][:, :, :],
                                    in_=w13T[e].rearrange("(c p) f -> p c f", p=128)).then_inc(s_we[slot].h, 16)
                        s.dma_start(out=w2_slot(slot)[:, :, :],
                                    in_=w2T[e].rearrange("(c p) d2 -> p c d2", p=128)).then_inc(s_we[slot].h, 16)
                    else:
                        s.dma_start(out=w13_slot[slot][:, :, :],
                                    in_=w13sT.rearrange("(c p) f -> p c f", p=128)).then_inc(s_we[slot].h, 16)
                        s.dma_start(out=w2_slot(slot)[:, :, :],
                                    in_=w2sT.rearrange("(c p) d2 -> p c d2", p=128)).then_inc(s_we[slot].h, 16)
                s_we[slot].inc(32)
                block.sync(_)

            emit_expert_load(0)
            emit_expert_load(1)

            for e in range(NE + 1):
                slot = e % 2
                is_shared = (e == NE)
                w13s_ = w13_slot[slot]
                w2s_ = w2_slot(slot)
                we_gate = s_we[slot].n

                if not is_shared:
                    ve_g = s_ve.n
                    sc_g = s_sc.n
                    def _(t, e=e, ve_g=ve_g, sc_g=sc_g):
                        t.wait_ge(s_sc.h, comb_ready_sc)
                        t.wait_ge(s_sc.h, sc_g)
                        t.wait_ge(s_ve.h, ve_g)
                        t.matmul(P[6][:, :], sel_sb[:, e, :], comb_em[:, 0:512], start=True, stop=True)
                        r = t.matmul(P[7][:, :], sel_sb[:, e, :], comb_em[:, 512:1024], start=True, stop=True)
                        r.then_inc(s_mm.h, 1)
                    block.tensor(_)
                    s_mm.inc(1)
                    mm_g = s_mm.n
                    def _(s, mm_g=mm_g):
                        s.wait_ge(s_mm.h, mm_g)
                        s.copy(out=scr[:, 3072:3584], in_=P[6][:, :])
                        r = s.copy(out=scr[:, 3584:4096], in_=P[7][:, :])
                        r.then_inc(s_sc.h, 1)
                    block.scalar(_)
                    s_sc.inc(1)
                cb_ready = s_sc.n

                for g in range(NG):
                    ve_g = s_ve.n
                    def _(t, g=g, we_gate=we_gate, cb_ready=cb_ready, ve_g=ve_g, w13s_=w13s_, slot=slot):
                        t.wait_ge(s_we[slot].h, we_gate)
                        t.wait_ge(s_sc.h, cb_ready)
                        t.wait_ge(s_ve.h, ve_g)
                        for fc in range(4):
                            for c in range(8):
                                r = t.matmul(P[fc][:, :], w13s_[:, c, fc * 128:(fc + 1) * 128],
                                             xn_sb[:, c, g * 512:(g + 1) * 512],
                                             start=(c == 0), stop=(c == 7))
                            r.then_inc(s_mm.h, 1)
                    block.tensor(_)
                    s_mm.inc(4)
                    mm_gate_half = s_mm.n
                    def _(s, g=g, mm_gate_half=mm_gate_half):
                        for fc in range(4):
                            s.wait_ge(s_mm.h, mm_gate_half - 3 + fc)
                            r = s.activation(out=ring[:, 2 * fc + g, :], in_=P[fc][:, :], func=AF.Silu)
                        r.then_inc(s_sc.h, 1)
                    block.scalar(_)
                    s_sc.inc(1)
                    def _(t, g=g, w13s_=w13s_):
                        for fc in range(4):
                            for c in range(8):
                                r = t.matmul(P[4 + fc][:, :], w13s_[:, c, 512 + fc * 128: 512 + (fc + 1) * 128],
                                             xn_sb[:, c, g * 512:(g + 1) * 512],
                                             start=(c == 0), stop=(c == 7))
                            r.then_inc(s_mm.h, 1)
                    block.tensor(_)
                    s_mm.inc(4)
                    mm_up = s_mm.n
                    sc_silu = s_sc.n
                    def _(v, g=g, mm_up=mm_up, is_shared=is_shared, sc_silu=sc_silu):
                        v.wait_ge(s_sc.h, sc_silu)
                        for fc in range(4):
                            v.wait_ge(s_mm.h, mm_up - 3 + fc)
                            v.tensor_tensor(out=ring[:, 2 * fc + g, :],
                                            in0=ring[:, 2 * fc + g, :],
                                            in1=P[4 + fc][:, :], op=OP.mult)
                            if not is_shared:
                                v.tensor_tensor(out=ring[:, 2 * fc + g, :],
                                                in0=ring[:, 2 * fc + g, :],
                                                in1=scr[:, (6 + g) * 512:(7 + g) * 512], op=OP.mult)
                        r = v.engine_nop()
                        r.then_inc(s_ve.h, 1)
                    block.vector(_)
                    s_ve.inc(1)
                    h_ready = s_ve.n
                    def _(t, g=g, h_ready=h_ready, w2s_=w2s_):
                        t.wait_ge(s_ve.h, h_ready)
                        for db in range(8):
                            for fc in range(4):
                                r = t.matmul(P[db][:, :], w2s_[:, fc, db * 128:(db + 1) * 128],
                                             ring[:, 2 * fc + g, :],
                                             start=(fc == 0), stop=(fc == 3))
                            r.then_inc(s_mm.h, 1)
                    block.tensor(_)
                    s_mm.inc(8)
                    mm_y = s_mm.n
                    def _(v, g=g, mm_y=mm_y):
                        for db in range(8):
                            v.wait_ge(s_mm.h, mm_y - 7 + db)
                            r = v.tensor_tensor(out=x_sb[:, db, g * 512:(g + 1) * 512],
                                                in0=x_sb[:, db, g * 512:(g + 1) * 512],
                                                in1=P[db][:, :], op=OP.add)
                        r.then_inc(s_ve.h, 1)
                    block.vector(_)
                    s_ve.inc(1)

                slot_gate_mm[slot] = s_mm.n
                if e + 2 <= NE:
                    emit_expert_load(e + 2)

            x_done = s_ve.n
            def _(g, t0=t0, t1=t1, x_done=x_done):
                g.wait_ge(s_ve.h, x_done)
                g.dma_start(out=outT[:, t0:t1].rearrange("(c p) t -> p c t", p=128),
                            in_=x_sb[:, :, :]).then_inc(s_xout.h, 16)
            block.gpsimd(_)
            s_xout.inc(16)

        xo_final = s_xout.n
        def _(g, xo_final=xo_final):
            g.wait_ge(s_xout.h, xo_final)
        block.gpsimd(_)

    es0.close()
    return nc


_NC_CACHE = None


def kernel(L0_attn_u, L0_attn_W_in, L0_attn_W_out, L0_attn_Wp, L0_attn_bp,
           L0_attn_alpha_log, L0_attn_delta_bias, L0_attn_log_gamma,
           L0_ffn_input_ids, L0_ffn_gate_w, L0_ffn_gate_b, L0_ffn_w13, L0_ffn_w2,
           L0_ffn_w13_s, L0_ffn_w2_s):
    global _NC_CACHE
    u = np.asarray(L0_attn_u, dtype=np.float32)

    c = np.ascontiguousarray
    w_inT = c(np.asarray(L0_attn_W_in, np.float32).T)
    w_outT = c(np.asarray(L0_attn_W_out, np.float32).T)
    wpT = c(np.asarray(L0_attn_Wp, np.float32).T)
    gate_wT = c(np.asarray(L0_ffn_gate_w, np.float32).T)
    gate_b = c(np.asarray(L0_ffn_gate_b, np.float32).reshape(NE, 1))
    w13T = c(np.asarray(L0_ffn_w13, np.float32).transpose(0, 2, 1))
    w2T = c(np.asarray(L0_ffn_w2, np.float32).transpose(0, 2, 1))
    w13sT = c(np.asarray(L0_ffn_w13_s, np.float32).T)
    w2sT = c(np.asarray(L0_ffn_w2_s, np.float32).T)
    alpha_log = c(np.asarray(L0_attn_alpha_log, np.float32).reshape(8, 128).T)
    delta_bias = c(np.asarray(L0_attn_delta_bias, np.float32).reshape(8, 128).T)
    bp = c(np.asarray(L0_attn_bp, np.float32).reshape(3, 8, 128).transpose(2, 0, 1).reshape(128, 24))
    log_gamma = c(np.asarray(L0_attn_log_gamma, np.float32).reshape(1, 1))
    eye = np.eye(128, dtype=np.float32)
    sel = np.zeros((NE, NE, 128), dtype=np.float32)
    for e in range(NE):
        sel[e, e, :] = 1.0

    shared = dict(w_inT=w_inT, w_outT=w_outT, wpT=wpT, gate_wT=gate_wT, gate_b=gate_b,
                  w13T=w13T, w2T=w2T, w13sT=w13sT, w2sT=w2sT, alpha_log=alpha_log,
                  delta_bias=delta_bias, bp=bp, log_gamma=log_gamma, eye=eye, sel_e=sel)

    in_maps = []
    for b in range(N_CORES):
        m = dict(shared)
        m["xT"] = c(u[b].T)
        in_maps.append(m)

    if _NC_CACHE is None:
        _NC_CACHE = build_program()
    nc = _NC_CACHE

    res = run_bass_kernel_spmd(nc, in_maps, core_ids=list(range(N_CORES)))
    out = np.stack([np.asarray(res.results[b]["outT"]).T for b in range(N_CORES)], axis=0)
    return out


# version 06
# speedup vs baseline: 1.0536x; 1.0536x over previous
import numpy as np
from contextlib import ExitStack

import concourse.bass as bass
import concourse.mybir as mybir
from concourse.bass_utils import run_bass_kernel_spmd

F32 = mybir.dt.float32
BF16 = mybir.dt.bfloat16
AF = mybir.ActivationFunctionType
OP = mybir.AluOpType
AX = mybir.AxisListType

N_CORES = 8
D = 1024
TC = 4096
TT = 1024
NTILE = TC // TT
NG = TT // 512
NE = 16
FI = 512
NSQ = 14
NPW = 6
SIGMA_STAGE = 99
DEBUG_SIGMA = False
EPS_A = 1e-4
LN_EPS = 1e-5


class Sem:
    def __init__(self, es, nc, name):
        self.h = es.enter_context(nc.semaphore(name))
        self.n = 0

    def inc(self, k=1):
        self.n += k
        return self.n


def emit_sigma(nc, block, es, wT, scale_out, consts, sems, tag):
    s_mm, s_ve, s_sc, s_ld = sems["mm"], sems["ve"], sems["sc"], sems["ld"]
    ones_col, ones_row, ones_bf = consts["ones_col"], consts["ones_row"], consts["ones_bf"]

    wt = es.enter_context(nc.sbuf_tensor(f"sg_wt{tag}", [128, 8, 1024], F32))
    Bf = es.enter_context(nc.sbuf_tensor(f"sg_Bf{tag}", [128, 8, 1024], F32))
    M0 = es.enter_context(nc.sbuf_tensor(f"sg_M0{tag}", [128, 8, 1024], BF16))
    M1 = es.enter_context(nc.sbuf_tensor(f"sg_M1{tag}", [128, 8, 1024], BF16))
    junk = es.enter_context(nc.sbuf_tensor(f"sg_junk{tag}", [128, 512], BF16))
    acc = es.enter_context(nc.sbuf_tensor(f"sg_acc{tag}", [128, 16], F32))
    r1 = es.enter_context(nc.sbuf_tensor(f"sg_r1{tag}", [128, 1], F32))
    inv1 = es.enter_context(nc.sbuf_tensor(f"sg_inv1{tag}", [128, 1], F32))
    ssc = es.enter_context(nc.sbuf_tensor(f"sg_ssc{tag}", [1, 1], F32))
    vv = es.enter_context(nc.sbuf_tensor(f"sg_vv{tag}", [128, 8], F32))
    tt = es.enter_context(nc.sbuf_tensor(f"sg_tt{tag}", [128, 8], F32))
    prod = es.enter_context(nc.sbuf_tensor(f"sg_prod{tag}", [128, 8], F32))
    den_sb = es.enter_context(nc.sbuf_tensor(f"sg_den{tag}", [128, 8], F32))
    n2 = es.enter_context(nc.sbuf_tensor(f"sg_n2{tag}", [1, 2], F32))
    ps0 = es.enter_context(nc.psum_tensor(f"sg_ps0{tag}", [128, 512], F32))
    ps1 = es.enter_context(nc.psum_tensor(f"sg_ps1{tag}", [128, 512], F32))
    pv = es.enter_context(nc.psum_tensor(f"sg_pv{tag}", [128, 1], F32))
    pv2 = es.enter_context(nc.psum_tensor(f"sg_pv2{tag}", [128, 1], F32))

    ld_gate = s_ld.n + 16
    ve_prev = s_ve.n
    sc_prev = s_sc.n
    mm_prev = s_mm.n

    def _(s):
        s.wait_ge(s_ve.h, ve_prev)
        s.wait_ge(s_sc.h, sc_prev)
        s.wait_ge(s_mm.h, mm_prev)
        s.dma_start(out=wt[:, :, :], in_=wT.rearrange("(c p) n -> p c n", p=128)).then_inc(s_ld.h, 16)
    block.sync(_)
    s_ld.inc(16)

    def sq_pass(src, dst, is_b_build):
        mm_base = s_mm.n
        ve_base = s_ve.n
        sc_base = s_sc.n

        def _t(t):
            if is_b_build:
                t.wait_ge(s_ld.h, ld_gate)
            t.wait_ge(s_ve.h, ve_base)
            t.wait_ge(s_sc.h, sc_base)
            for k in range(16):
                iblk, jg = k // 2, k % 2
                ps = ps0 if jg == 0 else ps1
                if k >= 2:
                    t.wait_ge(s_ve.h, ve_base + k - 1)
                    t.wait_ge(s_sc.h, sc_base + k - 1)
                for d in range(8):
                    r = t.matmul(ps[:, :], src[:, d, iblk * 128:(iblk + 1) * 128],
                                 src[:, d, jg * 512:(jg + 1) * 512],
                                 start=(d == 0), stop=(d == 7))
                r.then_inc(s_mm.h, 1)
        block.tensor(_t)
        s_mm.inc(16)

        def _v(v):
            for k in range(16):
                iblk, jg = k // 2, k % 2
                ps = ps0 if jg == 0 else ps1
                v.wait_ge(s_mm.h, mm_base + k + 1)
                tgt = Bf if is_b_build else dst
                r = v.tensor_copy(out=tgt[:, iblk, jg * 512:(jg + 1) * 512], in_=ps[:, :])
                r.then_inc(s_ve.h, 1)
        block.vector(_v)
        s_ve.inc(16)

        def _s(s):
            for k in range(16):
                iblk, jg = k // 2, k % 2
                s.wait_ge(s_ve.h, ve_base + k + 1)
                src_sb = (Bf if is_b_build else dst)[:, iblk, jg * 512:(jg + 1) * 512]
                if is_b_build:
                    r = s.activation(out=M0[:, iblk, jg * 512:(jg + 1) * 512], in_=src_sb,
                                     func=AF.Copy, bias=0.0)
                else:
                    r = s.activation(out=junk[:, :], in_=src_sb, func=AF.Abs,
                                     accum_out=acc[:, k:k + 1])
                r.then_inc(s_sc.h, 1)
        block.scalar(_s)
        s_sc.inc(16)

    sq_pass(wt, None, True)

    if SIGMA_STAGE < 2:
        return
    bufs = [M0, M1]
    for it in range(NSQ):
        src, dst = bufs[it % 2], bufs[(it + 1) % 2]
        sq_pass(src, dst, False)

        sc_gate = s_sc.n
        def _v2(v, sc_gate=sc_gate):
            v.wait_ge(s_sc.h, sc_gate)
            r = v.tensor_reduce(out=r1[:, :], in_=acc[:, :], axis=AX.X, op=OP.add)
            r.then_inc(s_ve.h, 1)
        block.vector(_v2)
        s_ve.inc(1)
        ve_g = s_ve.n
        def _t2(t, ve_g=ve_g):
            t.wait_ge(s_ve.h, ve_g)
            r = t.matmul(pv[:1, :], ones_col[:, :], r1[:, :], start=True, stop=True)
            r.then_inc(s_mm.h, 1)
        block.tensor(_t2)
        s_mm.inc(1)
        mm_g = s_mm.n
        def _v3(v, mm_g=mm_g):
            v.wait_ge(s_mm.h, mm_g)
            r = v.tensor_copy(out=ssc[:, :], in_=pv[:1, :])
            r.then_inc(s_ve.h, 1)
        block.vector(_v3)
        s_ve.inc(1)
        ve_g2 = s_ve.n
        def _t3(t, ve_g2=ve_g2):
            t.wait_ge(s_ve.h, ve_g2)
            r = t.matmul(pv2[:, :], ones_row[:1, :], ssc[:1, :], start=True, stop=True)
            r.then_inc(s_mm.h, 1)
        block.tensor(_t3)
        s_mm.inc(1)
        mm_g2 = s_mm.n
        def _v4(v, dst=dst, mm_g2=mm_g2):
            v.wait_ge(s_mm.h, mm_g2)
            v.reciprocal(out=inv1[:, :], in_=pv2[:, :])
            v.drain()
            for c in range(8):
                r = v.tensor_scalar(out=dst[:, c, :], in0=dst[:, c, :],
                                    scalar1=inv1[:, :], scalar2=256.0,
                                    op0=OP.mult, op1=OP.mult)
            r.then_inc(s_ve.h, 1)
        block.vector(_v4)
        s_ve.inc(1)

    Mfin = bufs[NSQ % 2]
    if SIGMA_STAGE < 3:
        return

    for pblk in range(8):
        gate = s_ve.n
        def _t(t, pblk=pblk, gate=gate):
            t.wait_ge(s_ve.h, gate)
            for q in range(8):
                r = t.matmul(pv[:, :], Mfin[:, q, pblk * 128:(pblk + 1) * 128],
                             ones_bf[:, :], start=(q == 0), stop=(q == 7))
            r.then_inc(s_mm.h, 1)
        block.tensor(_t)
        s_mm.inc(1)
        mm_g = s_mm.n
        def _v(v, pblk=pblk, mm_g=mm_g):
            v.wait_ge(s_mm.h, mm_g)
            r = v.tensor_copy(out=vv[:, pblk:pblk + 1], in_=pv[:, :])
            r.then_inc(s_ve.h, 1)
        block.vector(_v)
        s_ve.inc(1)

    if SIGMA_STAGE < 4:
        return
    vbuf, tbuf = vv, tt
    for step in range(NPW + 1):
        for pblk in range(8):
            gate = s_ve.n
            def _t(t, pblk=pblk, vbuf=vbuf, gate=gate):
                t.wait_ge(s_ve.h, gate)
                for q in range(8):
                    r = t.matmul(pv[:, :], Bf[:, q, pblk * 128:(pblk + 1) * 128],
                                 vbuf[:, q:q + 1], start=(q == 0), stop=(q == 7))
                r.then_inc(s_mm.h, 1)
            block.tensor(_t)
            s_mm.inc(1)
            mm_g = s_mm.n
            def _v(v, pblk=pblk, tbuf=tbuf, mm_g=mm_g):
                v.wait_ge(s_mm.h, mm_g)
                r = v.tensor_copy(out=tbuf[:, pblk:pblk + 1], in_=pv[:, :])
                r.then_inc(s_ve.h, 1)
            block.vector(_v)
            s_ve.inc(1)
        vbuf, tbuf = tbuf, vbuf
    vlast, tlast = tbuf, vbuf

    if SIGMA_STAGE < 5:
        return
    def _v5(v):
        v.tensor_tensor(out=prod[:, :], in0=vlast[:, :], in1=tlast[:, :], op=OP.mult)
        r = v.tensor_tensor(out=den_sb[:, :], in0=vlast[:, :], in1=vlast[:, :], op=OP.mult)
        r.then_inc(s_ve.h, 1)
    block.vector(_v5)
    s_ve.inc(1)

    ve_g = s_ve.n
    def _t4(t, ve_g=ve_g):
        t.wait_ge(s_ve.h, ve_g)
        for q in range(8):
            t.matmul(pv[:1, :], ones_col[:, :], prod[:, q:q + 1], start=(q == 0), stop=(q == 7))
        for q in range(8):
            r = t.matmul(pv2[:1, :], ones_col[:, :], den_sb[:, q:q + 1], start=(q == 0), stop=(q == 7))
        r.then_inc(s_mm.h, 1)
    block.tensor(_t4)
    s_mm.inc(1)

    mm_g = s_mm.n
    def _v6(v, mm_g=mm_g):
        v.wait_ge(s_mm.h, mm_g)
        v.tensor_copy(out=n2[:, 0:1], in_=pv[:1, :])
        v.tensor_copy(out=n2[:, 1:2], in_=pv2[:1, :])
        v.drain()
        v.reciprocal(out=n2[:, 1:2], in_=n2[:, 1:2])
        v.drain()
        r = v.tensor_tensor(out=scale_out[:, :], in0=n2[:, 0:1], in1=n2[:, 1:2], op=OP.mult)
        r.then_inc(s_ve.h, 1)
    block.vector(_v6)
    s_ve.inc(1)

    ve_g = s_ve.n
    def _s2(s, ve_g=ve_g):
        s.wait_ge(s_ve.h, ve_g)
        r = s.activation(out=scale_out[:, :], in_=scale_out[:, :], func=AF.Sqrt)
        r.then_inc(s_sc.h, 1)
    block.scalar(_s2)
    s_sc.inc(1)
    sc_g = s_sc.n
    def _v7(v, sc_g=sc_g):
        v.wait_ge(s_sc.h, sc_g)
        r = v.tensor_scalar(out=scale_out[:, :], in0=scale_out[:, :], scalar1=1.0,
                            scalar2=None, op0=OP.max)
        r.then_inc(s_ve.h, 1)
    block.vector(_v7)
    s_ve.inc(1)


def build_program():
    nc = bass.Bass("TRN2", target_bir_lowering=False, detect_race_conditions=False)

    xT = nc.dram_tensor("xT", [D, TC], F32, kind="ExternalInput").ap()
    outT = nc.dram_tensor("outT", [D, TC], F32, kind="ExternalOutput").ap()
    w_inT = nc.dram_tensor("w_inT", [D, D], F32, kind="ExternalInput").ap()
    w_outT = nc.dram_tensor("w_outT", [D, D], F32, kind="ExternalInput").ap()
    wpT = nc.dram_tensor("wpT", [D, 3 * D], F32, kind="ExternalInput").ap()
    gate_wT = nc.dram_tensor("gate_wT", [D, NE], F32, kind="ExternalInput").ap()
    gate_b = nc.dram_tensor("gate_b", [NE, 1], F32, kind="ExternalInput").ap()
    w13T = nc.dram_tensor("w13T", [NE, D, 2 * FI], F32, kind="ExternalInput").ap()
    w2T = nc.dram_tensor("w2T", [NE, FI, D], F32, kind="ExternalInput").ap()
    w13sT = nc.dram_tensor("w13sT", [D, 2 * FI], F32, kind="ExternalInput").ap()
    w2sT = nc.dram_tensor("w2sT", [FI, D], F32, kind="ExternalInput").ap()
    alpha_log = nc.dram_tensor("alpha_log", [128, 8], F32, kind="ExternalInput").ap()
    delta_bias = nc.dram_tensor("delta_bias", [128, 8], F32, kind="ExternalInput").ap()
    bp = nc.dram_tensor("bp", [128, 24], F32, kind="ExternalInput").ap()
    log_gamma = nc.dram_tensor("log_gamma", [1, 1], F32, kind="ExternalInput").ap()
    eye = nc.dram_tensor("eye", [128, 128], F32, kind="ExternalInput").ap()
    sel_e = nc.dram_tensor("sel_e", [NE, NE, 128], F32, kind="ExternalInput").ap()

    es0 = ExitStack()
    eye_sb = es0.enter_context(nc.sbuf_tensor("eye_sb", [128, 128], F32))
    ones_col = es0.enter_context(nc.sbuf_tensor("ones_col", [128, 1], F32))
    ones_row = es0.enter_context(nc.sbuf_tensor("ones_row", [1, 128], F32))
    ones_bf = es0.enter_context(nc.sbuf_tensor("ones_bf", [128, 1], BF16))
    scale1 = es0.enter_context(nc.sbuf_tensor("scale1", [1, 1], F32))
    scale2 = es0.enter_context(nc.sbuf_tensor("scale2", [1, 1], F32))
    attn_scale = es0.enter_context(nc.sbuf_tensor("attn_scale", [128, 1], F32))
    lg_sb = es0.enter_context(nc.sbuf_tensor("lg_sb", [1, 1], F32))
    alpha_n = es0.enter_context(nc.sbuf_tensor("alpha_n", [128, 8], F32))
    db_comb = es0.enter_context(nc.sbuf_tensor("db_comb", [128, 8], F32))
    bp_b = es0.enter_context(nc.sbuf_tensor("bp_b", [128, 8], F32))
    bp_c = es0.enter_context(nc.sbuf_tensor("bp_c", [128, 8], F32))
    gate_b_sb = es0.enter_context(nc.sbuf_tensor("gate_b_sb", [NE, 1], F32))
    gb_bcast = es0.enter_context(nc.sbuf_tensor("gb_bcast", [128, NE], F32))
    sel_sb = es0.enter_context(nc.sbuf_tensor("sel_sb", [NE, NE, 128], F32))
    gwt = es0.enter_context(nc.sbuf_tensor("gwt", [128, 8, NE], F32))
    carry = es0.enter_context(nc.sbuf_tensor("carry", [128, 8], F32))
    tmp8 = es0.enter_context(nc.sbuf_tensor("tmp8", [128, 8], F32))
    cpool = es0.enter_context(nc.sbuf_tensor("cpool", [128, 2], F32))
    Mcol = es0.enter_context(nc.sbuf_tensor("Mcol", [128, 1], F32))
    negM = es0.enter_context(nc.sbuf_tensor("negM", [128, 1], F32))

    sems = {}
    for name in ["ld", "mm", "ve", "sc", "xin", "xout", "wa", "we0", "we1"]:
        sems[name] = Sem(es0, nc, "s_" + name)
    s_ld, s_mm, s_ve, s_sc = sems["ld"], sems["mm"], sems["ve"], sems["sc"]
    s_xin, s_xout, s_wa = sems["xin"], sems["xout"], sems["wa"]
    s_we = [sems["we0"], sems["we1"]]

    consts = {"ones_col": ones_col, "ones_row": ones_row, "ones_bf": ones_bf}

    with ExitStack() as esA:
        block = esA.enter_context(nc.Block())

        def _(s):
            s.dma_start(out=eye_sb[:, :], in_=eye).then_inc(s_ld.h, 16)
            s.dma_start(out=lg_sb[:, :], in_=log_gamma).then_inc(s_ld.h, 16)
            s.dma_start(out=alpha_n[:, :], in_=alpha_log).then_inc(s_ld.h, 16)
            s.dma_start(out=db_comb[:, :], in_=delta_bias).then_inc(s_ld.h, 16)
            s.dma_start(out=tmp8[:, :], in_=bp[:, 0:8]).then_inc(s_ld.h, 16)
            s.dma_start(out=bp_b[:, :], in_=bp[:, 8:16]).then_inc(s_ld.h, 16)
            s.dma_start(out=bp_c[:, :], in_=bp[:, 16:24]).then_inc(s_ld.h, 16)
            s.dma_start(out=gate_b_sb[:, :], in_=gate_b).then_inc(s_ld.h, 16)
            s.dma_start(out=gwt[:, :, :], in_=gate_wT.rearrange("(c p) e -> p c e", p=128)).then_inc(s_ld.h, 16)
            s.dma_start(out=sel_sb[:, :, :], in_=sel_e.rearrange("e k j -> k e j")).then_inc(s_ld.h, 16)
        block.sync(_)
        s_ld.inc(16 * 10)

        ld_g = s_ld.n
        def _(v, ld_g=ld_g):
            v.wait_ge(s_ld.h, ld_g)
            v.memset(ones_col[:, :], 1.0)
            v.memset(ones_row[:, :], 1.0)
            v.memset(ones_bf[:, :], 1.0)
            v.memset(carry[:, :], 0.0)
            v.memset(cpool[:, 0:1], 100000.0)
            v.drain()
            v.reciprocal(out=cpool[:, 0:1], in_=cpool[:, 0:1])
            v.drain()
            v.memset(cpool[:, 1:2], 1e-12)
            v.memset(Mcol[:, :], 1.0)
            v.drain()
            v.tensor_scalar(out=Mcol[:, :], in0=Mcol[:, :], scalar1=float(2.0 ** -29),
                            scalar2=1.0, op0=OP.add, op1=OP.subtract)
            v.drain()
            v.tensor_scalar(out=Mcol[:, :], in0=Mcol[:, :], scalar1=float(2.0 ** 58),
                            scalar2=None, op0=OP.mult)
            v.drain()
            v.tensor_scalar(out=negM[:, :], in0=Mcol[:, :], scalar1=-1.0, scalar2=None, op0=OP.mult)
            v.drain()
            r = v.tensor_tensor(out=db_comb[:, :], in0=db_comb[:, :], in1=tmp8[:, :], op=OP.add)
            r.then_inc(s_ve.h, 1)
        block.vector(_)
        s_ve.inc(1)
        def _(s, ld_g=ld_g):
            s.wait_ge(s_ld.h, ld_g)
            s.activation(out=alpha_n[:, :], in_=alpha_n[:, :], func=AF.Exp)
            s.add(alpha_n[:, :], alpha_n[:, :], 1.0)
            s.activation(out=alpha_n[:, :], in_=alpha_n[:, :], func=AF.Ln)
            r = s.mul(alpha_n[:, :], alpha_n[:, :], -1.0)
            r.then_inc(s_sc.h, 1)
        block.scalar(_)
        s_sc.inc(1)

        with ExitStack() as esS:
            emit_sigma(nc, block, esS, w_inT, scale1, consts, sems, "a")
        with ExitStack() as esS:
            emit_sigma(nc, block, esS, w_outT, scale2, consts, sems, "b")

        with ExitStack() as esS:
            pvb = esS.enter_context(nc.psum_tensor("pvb", [128, 1], F32))
            n2gb = esS.enter_context(nc.sbuf_tensor("n2gb", [1, NE], F32))
            ve_g = s_ve.n
            def _(s, ve_g=ve_g):
                s.wait_ge(s_ve.h, ve_g)
                r = s.activation(out=lg_sb[:, :], in_=lg_sb[:, :], func=AF.Exp)
                r.then_inc(s_sc.h, 1)
            block.scalar(_)
            s_sc.inc(1)
            sc_g = s_sc.n
            def _(v, sc_g=sc_g):
                v.wait_ge(s_sc.h, sc_g)
                v.tensor_tensor(out=scale1[:, :], in0=scale1[:, :], in1=scale2[:, :], op=OP.mult)
                v.drain()
                v.reciprocal(out=scale1[:, :], in_=scale1[:, :])
                v.drain()
                r = v.tensor_tensor(out=scale1[:, :], in0=scale1[:, :], in1=lg_sb[:, :], op=OP.mult)
                r.then_inc(s_ve.h, 1)
            block.vector(_)
            s_ve.inc(1)
            ve_g = s_ve.n
            def _(t, ve_g=ve_g):
                t.wait_ge(s_ve.h, ve_g)
                r = t.matmul(pvb[:, :], ones_row[:1, :], scale1[:1, :], start=True, stop=True)
                r.then_inc(s_mm.h, 1)
            block.tensor(_)
            s_mm.inc(1)
            mm_g = s_mm.n
            def _(v, mm_g=mm_g):
                v.wait_ge(s_mm.h, mm_g)
                r = v.tensor_copy(out=attn_scale[:, :], in_=pvb[:, :])
                r.then_inc(s_ve.h, 1)
            block.vector(_)
            s_ve.inc(1)
            pgb = esS.enter_context(nc.psum_tensor("pgb", [128, NE], F32))
            ve_g = s_ve.n
            def _(t, ve_g=ve_g):
                t.wait_ge(s_ve.h, ve_g)
                r = t.transpose(pgb[:1, :NE], gate_b_sb[:, :], eye_sb[:NE, :NE])
                r.then_inc(s_mm.h, 1)
            block.tensor(_)
            s_mm.inc(1)
            mm_g = s_mm.n
            def _(v, mm_g=mm_g):
                v.wait_ge(s_mm.h, mm_g)
                r = v.tensor_copy(out=lg_sb[:, :] if False else n2gb[:1, :], in_=pgb[:1, :NE])
                r.then_inc(s_ve.h, 1)
            block.vector(_)
            s_ve.inc(1)
            ve_g = s_ve.n
            def _(t, ve_g=ve_g):
                t.wait_ge(s_ve.h, ve_g)
                r = t.matmul(pgb[:, :], ones_row[:1, :], n2gb[:1, :], start=True, stop=True)
                r.then_inc(s_mm.h, 1)
            block.tensor(_)
            s_mm.inc(1)
            mm_g = s_mm.n
            def _(v, mm_g=mm_g):
                v.wait_ge(s_mm.h, mm_g)
                r = v.tensor_copy(out=gb_bcast[:, :], in_=pgb[:, :])
                r.then_inc(s_ve.h, 1)
            block.vector(_)
            s_ve.inc(1)

    if DEBUG_SIGMA:
        with ExitStack() as esD:
            block = esD.enter_context(nc.Block())
            sD = Sem(esD, nc, "s_dbg")
            def _(g):
                g.dma_start(out=outT[0:1, 0:1], in_=attn_scale[0:1, :]).then_inc(sD.h, 16)
                g.dma_start(out=outT[0:1, 4:5], in_=scale1[:, :]).then_inc(sD.h, 16)
                g.dma_start(out=outT[0:1, 8:9], in_=scale2[:, :]).then_inc(sD.h, 16)
                g.wait_ge(sD.h, 48)
            block.gpsimd(_)

    with ExitStack() as esB:
        x_sb = esB.enter_context(nc.sbuf_tensor("x_sb", [128, 8, TT], F32))
        xn_sb = esB.enter_context(nc.sbuf_tensor("xn_sb", [128, 8, TT], F32))
        Ra = esB.enter_context(nc.sbuf_tensor("Ra", [128, 8, TT], F32))
        Rb = esB.enter_context(nc.sbuf_tensor("Rb", [128, 8, TT], F32))
        Rc = esB.enter_context(nc.sbuf_tensor("Rc", [128, 8, TT], F32))
        ring = esB.enter_context(nc.sbuf_tensor("ring", [128, 8, 512], F32))
        scr = esB.enter_context(nc.sbuf_tensor("scr", [128, 4096], F32))
        comb_em = esB.enter_context(nc.sbuf_tensor("comb_em", [NE, TT], F32))
        btok = esB.enter_context(nc.sbuf_tensor("btok", [128, 8, NE], F32))
        stok = esB.enter_context(nc.sbuf_tensor("stok", [128, 8, NE], F32))
        rtok = esB.enter_context(nc.sbuf_tensor("rtok", [128, 8, NE], F32))
        thr8 = esB.enter_context(nc.sbuf_tensor("thr8", [128, 8], F32))
        invd8 = esB.enter_context(nc.sbuf_tensor("invd8", [128, 8], F32))

        P = [esB.enter_context(nc.psum_tensor(f"P{i}", [128, 512], F32)) for i in range(8)]
        stats = [Rc[:1, 0, 0:512], Rc[:1, 0, 512:1024], Rc[:1, 1, 0:512],
                 Rc[:1, 1, 512:1024], Rc[:1, 2, 0:512]]
        scores_em = Rc[0:NE, 3, :]

        block = esB.enter_context(nc.Block())

        w13_slot = [Ra, Rb]

        def w2_slot(slot):
            return Rc[:, 4 * slot:4 * slot + 4, :]

        for j in range(NTILE):
            t0, t1 = j * TT, (j + 1) * TT

            xo_gate = s_xout.n
            def _(g, t0=t0, t1=t1, xo_gate=xo_gate):
                g.wait_ge(s_xout.h, xo_gate)
                g.dma_start(out=x_sb[:, :, :],
                            in_=xT[:, t0:t1].rearrange("(c p) t -> p c t", p=128)).then_inc(s_xin.h, 16)
            block.gpsimd(_)
            s_xin.inc(16)

            panels = ([("wp", wpT[:, p * 256:(p + 1) * 256]) for p in range(12)]
                      + [("win", w_inT[:, p * 256:(p + 1) * 256]) for p in range(4)]
                      + [("wout", w_outT[:, p * 256:(p + 1) * 256]) for p in range(4)])
            wa_base = s_wa.n

            def emit_panel_load(pi, gate_mm, gate_ve):
                def _(s, pi=pi, gate_mm=gate_mm, gate_ve=gate_ve):
                    if gate_mm is not None:
                        s.wait_ge(s_mm.h, gate_mm)
                    if gate_ve is not None:
                        s.wait_ge(s_ve.h, gate_ve)
                    slot = pi % 2
                    s.dma_start(out=ring[:, :, slot * 256:(slot + 1) * 256],
                                in_=panels[pi][1].rearrange("(c p) n -> p c n", p=128)).then_inc(s_wa.h, 16)
                block.sync(_)
                s_wa.inc(16)

            emit_panel_load(0, s_mm.n, s_ve.n)
            emit_panel_load(1, s_mm.n, s_ve.n)
            pi_loaded = 2

            xin_gate = s_xin.n
            for g in range(NG):
                sc_pre = s_sc.n
                mm_pre = s_mm.n
                ve_pre = s_ve.n
                def _(s, g=g, xin_gate=xin_gate, mm_pre=mm_pre):
                    s.wait_ge(s_xin.h, xin_gate)
                    if g > 0:
                        s.wait_ge(s_mm.h, mm_pre)
                    for c in range(8):
                        r = s.activation(out=scr[:, c * 512:(c + 1) * 512],
                                         in_=x_sb[:, c, g * 512:(g + 1) * 512], func=AF.Square)
                    r.then_inc(s_sc.h, 1)
                block.scalar(_)
                s_sc.inc(1)
                sc_sq = s_sc.n
                def _(t, g=g, xin_gate=xin_gate, sc_sq=sc_sq, ve_pre=ve_pre):
                    t.wait_ge(s_xin.h, xin_gate)
                    t.wait_ge(s_sc.h, sc_sq)
                    t.wait_ge(s_ve.h, ve_pre)
                    for c in range(8):
                        t.matmul(P[0][:1, :], ones_col[:, :], x_sb[:, c, g * 512:(g + 1) * 512],
                                 start=(c == 0), stop=(c == 7))
                    for c in range(8):
                        r = t.matmul(P[1][:1, :], ones_col[:, :], scr[:, c * 512:(c + 1) * 512],
                                     start=(c == 0), stop=(c == 7))
                    r.then_inc(s_mm.h, 1)
                block.tensor(_)
                s_mm.inc(1)
                mm_g = s_mm.n
                def _(v, mm_g=mm_g):
                    v.wait_ge(s_mm.h, mm_g)
                    v.tensor_scalar(out=stats[0], in0=P[0][:1, :], scalar1=1.0 / D,
                                    scalar2=None, op0=OP.mult)
                    v.tensor_scalar(out=stats[1], in0=P[1][:1, :], scalar1=1.0 / D,
                                    scalar2=None, op0=OP.mult)
                    v.drain()
                    v.tensor_tensor(out=stats[2], in0=stats[0], in1=stats[0], op=OP.mult)
                    v.drain()
                    r = v.tensor_tensor(out=stats[1], in0=stats[1], in1=stats[2], op=OP.subtract)
                    r.then_inc(s_ve.h, 1)
                block.vector(_)
                s_ve.inc(1)
                ve_g = s_ve.n
                def _(s, ve_g=ve_g):
                    s.wait_ge(s_ve.h, ve_g)
                    r = s.activation(out=stats[2], in_=stats[1], func=AF.Sqrt, bias=cpool[:1, 0:1])
                    r.then_inc(s_sc.h, 1)
                block.scalar(_)
                s_sc.inc(1)
                sc_g = s_sc.n
                def _(v, sc_g=sc_g):
                    v.wait_ge(s_sc.h, sc_g)
                    v.reciprocal(out=stats[3], in_=stats[2])
                    v.drain()
                    r = v.tensor_tensor(out=stats[4], in0=stats[0], in1=stats[3], op=OP.mult)
                    r.then_inc(s_ve.h, 1)
                block.vector(_)
                s_ve.inc(1)
                ve_g = s_ve.n
                def _(t, ve_g=ve_g):
                    t.wait_ge(s_ve.h, ve_g)
                    t.matmul(P[2][:, :], ones_row[:1, :], stats[3], start=True, stop=True)
                    r = t.matmul(P[3][:, :], ones_row[:1, :], stats[4], start=True, stop=True)
                    r.then_inc(s_mm.h, 1)
                block.tensor(_)
                s_mm.inc(1)
                mm_g = s_mm.n
                def _(v, g=g, mm_g=mm_g):
                    v.wait_ge(s_mm.h, mm_g)
                    for c in range(8):
                        v.tensor_tensor(out=xn_sb[:, c, g * 512:(g + 1) * 512],
                                        in0=x_sb[:, c, g * 512:(g + 1) * 512], in1=P[2][:, :], op=OP.mult)
                        r = v.tensor_tensor(out=xn_sb[:, c, g * 512:(g + 1) * 512],
                                            in0=xn_sb[:, c, g * 512:(g + 1) * 512], in1=P[3][:, :], op=OP.subtract)
                    r.then_inc(s_ve.h, 1)
                block.vector(_)
                s_ve.inc(1)
            xn_ready = s_ve.n

            for g in range(NG):
                mm_pre_rnd = s_mm.n
                def _(v, g=g, mm_pre_rnd=mm_pre_rnd):
                    if g == 0:
                        v.wait_ge(s_ve.h, xn_ready)
                    else:
                        v.wait_ge(s_mm.h, mm_pre_rnd)
                    for c in range(8):
                        v.scalar_tensor_tensor(out=scr[:, c * 512:(c + 1) * 512],
                                               in0=xn_sb[:, c, g * 512:(g + 1) * 512],
                                               scalar=Mcol[:, :],
                                               in1=xn_sb[:, c, g * 512:(g + 1) * 512],
                                               op0=OP.mult, op1=OP.add)
                    for c in range(8):
                        r = v.scalar_tensor_tensor(out=scr[:, c * 512:(c + 1) * 512],
                                                   in0=xn_sb[:, c, g * 512:(g + 1) * 512],
                                                   scalar=negM[:, :],
                                                   in1=scr[:, c * 512:(c + 1) * 512],
                                                   op0=OP.mult, op1=OP.add)
                    r.then_inc(s_ve.h, 1)
                block.vector(_)
                s_ve.inc(1)
                rnd_ready = s_ve.n
                sc_pre = s_sc.n
                def _(t, g=g, sc_pre=sc_pre, rnd_ready=rnd_ready):
                    t.wait_ge(s_ve.h, rnd_ready)
                    t.wait_ge(s_sc.h, sc_pre)
                    for c in range(8):
                        r = t.matmul(P[4][:NE, :], gwt[:, c, :], scr[:, c * 512:(c + 1) * 512],
                                     start=(c == 0), stop=(c == 7))
                    r.then_inc(s_mm.h, 1)
                block.tensor(_)
                s_mm.inc(1)
                mm_g = s_mm.n
                def _(s, g=g, mm_g=mm_g):
                    s.wait_ge(s_mm.h, mm_g)
                    r = s.activation(out=scores_em[:, g * 512:(g + 1) * 512], in_=P[4][:NE, :], func=AF.Sigmoid)
                    r.then_inc(s_sc.h, 1)
                block.scalar(_)
                s_sc.inc(1)
            sc_scores = s_sc.n
            for sub in range(8):
                sc_g = s_sc.n
                def _(t, sub=sub, sc_g=sc_g):
                    t.wait_ge(s_sc.h, sc_scores)
                    if sub > 0:
                        t.wait_ge(s_sc.h, sc_g)
                    r = t.transpose(P[6][:, :NE], scores_em[:, sub * 128:(sub + 1) * 128], eye_sb[:NE, :NE])
                    r.then_inc(s_mm.h, 1)
                block.tensor(_)
                s_mm.inc(1)
                mm_g = s_mm.n
                def _(s, sub=sub, mm_g=mm_g):
                    s.wait_ge(s_mm.h, mm_g)
                    r = s.copy(out=stok[:, sub, :], in_=P[6][:, :NE])
                    r.then_inc(s_sc.h, 1)
                block.scalar(_)
                s_sc.inc(1)
            sc_routing = s_sc.n
            def _routing(v, sc_routing=sc_routing):
                v.wait_ge(s_sc.h, sc_routing)
                for sub in range(8):
                    v.tensor_tensor(out=btok[:, sub, :], in0=stok[:, sub, :], in1=gb_bcast[:, :], op=OP.add)
                for sub in range(8):
                    v.tensor_copy(out=rtok[:, sub, :], in_=btok[:, sub, :])
                for rnd in range(4):
                    for sub in range(8):
                        v.tensor_reduce(out=thr8[:, sub:sub + 1], in_=rtok[:, sub, :], axis=AX.X, op=OP.max)
                    if rnd < 3:
                        for sub in range(8):
                            v.tensor_scalar(out=scr[:, sub * 16:(sub + 1) * 16], in0=rtok[:, sub, :],
                                            scalar1=thr8[:, sub:sub + 1], scalar2=None, op0=OP.is_ge)
                        for sub in range(8):
                            v.scalar_tensor_tensor(out=rtok[:, sub, :], in0=scr[:, sub * 16:(sub + 1) * 16],
                                                   scalar=-1e30, in1=rtok[:, sub, :], op0=OP.mult, op1=OP.add)
                for sub in range(8):
                    v.tensor_scalar(out=scr[:, sub * 16:(sub + 1) * 16], in0=btok[:, sub, :],
                                    scalar1=thr8[:, sub:sub + 1], scalar2=None, op0=OP.is_ge)
                for sub in range(8):
                    v.tensor_tensor(out=rtok[:, sub, :], in0=stok[:, sub, :],
                                    in1=scr[:, sub * 16:(sub + 1) * 16], op=OP.mult)
                for sub in range(8):
                    v.tensor_reduce(out=invd8[:, sub:sub + 1], in_=rtok[:, sub, :], axis=AX.X, op=OP.add)
                v.drain()
                v.reciprocal(out=invd8[:, :], in_=invd8[:, :])
                v.drain()
                for sub in range(8):
                    v.tensor_scalar(out=btok[:, sub, :], in0=rtok[:, sub, :],
                                    scalar1=invd8[:, sub:sub + 1],
                                    scalar2=2.5, op0=OP.mult, op1=OP.mult)
                v.drain()
                v.tensor_copy(out=rtok[:, :, :], in_=btok[:, :, :])
                v.drain()
                v.scalar_tensor_tensor(out=btok[:, :, :], in0=rtok[:, :, :], scalar=Mcol[:, :],
                                       in1=rtok[:, :, :], op0=OP.mult, op1=OP.add)
                v.drain()
                r = v.scalar_tensor_tensor(out=btok[:, :, :], in0=rtok[:, :, :], scalar=negM[:, :],
                                           in1=btok[:, :, :], op0=OP.mult, op1=OP.add)
                r.then_inc(s_ve.h, 1)
            block.vector(_routing)
            s_ve.inc(1)
            for sub in range(8):
                ve_g = s_ve.n
                sc_g = s_sc.n
                def _(t, sub=sub, ve_g=ve_g, sc_g=sc_g):
                    t.wait_ge(s_ve.h, ve_g)
                    if sub > 0:
                        t.wait_ge(s_sc.h, sc_g)
                    r = t.transpose(P[5][:NE, :128], btok[:, sub, :], eye_sb[:, :])
                    r.then_inc(s_mm.h, 1)
                block.tensor(_)
                s_mm.inc(1)
                mm_g = s_mm.n
                def _(s, sub=sub, mm_g=mm_g):
                    s.wait_ge(s_mm.h, mm_g)
                    r = s.copy(out=comb_em[:, sub * 128:(sub + 1) * 128], in_=P[5][:NE, :128])
                    r.then_inc(s_sc.h, 1)
                block.scalar(_)
                s_sc.inc(1)
            comb_ready_sc = s_sc.n

            for pi in range(12):
                wa_gate = wa_base + 16 * (pi + 1)
                for blk in range(2):
                    kblk = pi * 2 + blk
                    sect = kblk // 8
                    loc = kblk % 8
                    for g in range(NG):
                        pband = P[2 * (kblk % 2) + g]
                        ve_g = s_ve.n
                        sc_g = s_sc.n
                        def _(t, pi=pi, blk=blk, g=g, wa_gate=wa_gate, pband=pband,
                              ve_g=ve_g, sc_g=sc_g):
                            t.wait_ge(s_wa.h, wa_gate)
                            t.wait_ge(s_ve.h, ve_g)
                            t.wait_ge(s_sc.h, sc_g)
                            slot = pi % 2
                            for c in range(8):
                                r = t.matmul(pband[:, :],
                                             ring[:, c, slot * 256 + blk * 128: slot * 256 + (blk + 1) * 128],
                                             xn_sb[:, c, g * 512:(g + 1) * 512],
                                             start=(c == 0), stop=(c == 7))
                            r.then_inc(s_mm.h, 1)
                        block.tensor(_)
                        s_mm.inc(1)
                        mm_g = s_mm.n
                        if sect == 0:
                            def _(s, loc=loc, g=g, pband=pband, mm_g=mm_g):
                                s.wait_ge(s_mm.h, mm_g)
                                s.activation(out=scr[:, 0:512], in_=pband[:, :], func=AF.Exp,
                                             bias=db_comb[:, loc:loc + 1])
                                s.add(scr[:, 0:512], scr[:, 0:512], 1.0)
                                s.activation(out=scr[:, 0:512], in_=scr[:, 0:512], func=AF.Ln)
                                r = s.activation(out=Ra[:, loc, g * 512:(g + 1) * 512], in_=scr[:, 0:512],
                                                 func=AF.Exp, scale=alpha_n[:, loc:loc + 1])
                                r.then_inc(s_sc.h, 1)
                            block.scalar(_)
                            s_sc.inc(1)
                            sc_g = s_sc.n
                            def _(v, loc=loc, g=g, sc_g=sc_g):
                                v.wait_ge(s_sc.h, sc_g)
                                r = v.tensor_scalar(out=Ra[:, loc, g * 512:(g + 1) * 512],
                                                    in0=Ra[:, loc, g * 512:(g + 1) * 512],
                                                    scalar1=1.0 - EPS_A, scalar2=None, op0=OP.min)
                                r.then_inc(s_ve.h, 1)
                            block.vector(_)
                            s_ve.inc(1)
                        else:
                            dst = Rb if sect == 1 else Rc
                            bias_t = bp_b if sect == 1 else bp_c
                            def _(s, loc=loc, g=g, pband=pband, dst=dst, bias_t=bias_t, mm_g=mm_g):
                                s.wait_ge(s_mm.h, mm_g)
                                r = s.activation(out=dst[:, loc, g * 512:(g + 1) * 512], in_=pband[:, :],
                                                 func=AF.Tanh, bias=bias_t[:, loc:loc + 1])
                                r.then_inc(s_sc.h, 1)
                            block.scalar(_)
                            s_sc.inc(1)
                if pi_loaded < 20:
                    emit_panel_load(pi_loaded, s_mm.n, None)
                    pi_loaded += 1

            for c in range(8):
                for g in range(NG):
                    sl = slice(g * 512, (g + 1) * 512)
                    sc_g = s_sc.n
                    def _(v, c=c, sl=sl, sc_g=sc_g):
                        v.wait_ge(s_sc.h, sc_g)
                        a_, b_, c_ = Ra[:, c, sl], Rb[:, c, sl], Rc[:, c, sl]
                        p_, r_, q_, d_ = scr[:, 512:1024], scr[:, 1024:1536], scr[:, 1536:2048], scr[:, 2048:2560]
                        v.tensor_tensor(out=p_, in0=a_, in1=a_, op=OP.mult)
                        v.tensor_tensor(out=q_, in0=c_, in1=c_, op=OP.mult)
                        v.tensor_tensor(out=p_, in0=p_, in1=q_, op=OP.add)
                        v.tensor_tensor(out=r_, in0=b_, in1=b_, op=OP.mult)
                        v.tensor_tensor(out=q_, in0=a_, in1=b_, op=OP.mult)
                        v.tensor_tensor(out=d_, in0=p_, in1=r_, op=OP.subtract)
                        v.tensor_tensor(out=d_, in0=d_, in1=d_, op=OP.mult)
                        v.tensor_tensor(out=q_, in0=q_, in1=q_, op=OP.mult)
                        v.tensor_scalar(out=q_, in0=q_, scalar1=4.0, scalar2=None, op0=OP.mult)
                        v.tensor_tensor(out=d_, in0=d_, in1=q_, op=OP.add)
                        r2 = v.tensor_tensor(out=p_, in0=p_, in1=r_, op=OP.add)
                        r2.then_inc(s_ve.h, 1)
                    block.vector(_)
                    s_ve.inc(1)
                    ve_g = s_ve.n
                    def _(s, ve_g=ve_g):
                        s.wait_ge(s_ve.h, ve_g)
                        r2 = s.activation(out=scr[:, 2560:3072], in_=scr[:, 2048:2560], func=AF.Sqrt, bias=cpool[:, 1:2])
                        r2.then_inc(s_sc.h, 1)
                    block.scalar(_)
                    s_sc.inc(1)
                    sc_g = s_sc.n
                    def _(v, sc_g=sc_g):
                        v.wait_ge(s_sc.h, sc_g)
                        r2 = v.tensor_tensor(out=scr[:, 512:1024], in0=scr[:, 512:1024], in1=scr[:, 2560:3072], op=OP.add)
                        r2.then_inc(s_ve.h, 1)
                    block.vector(_)
                    s_ve.inc(1)
                    ve_g = s_ve.n
                    def _(s, ve_g=ve_g):
                        s.wait_ge(s_ve.h, ve_g)
                        r2 = s.activation(out=scr[:, 512:1024], in_=scr[:, 512:1024], func=AF.Sqrt, scale=0.5, bias=cpool[:, 1:2])
                        r2.then_inc(s_sc.h, 1)
                    block.scalar(_)
                    s_sc.inc(1)
                    sc_g = s_sc.n
                    def _(v, c=c, sl=sl, sc_g=sc_g):
                        v.wait_ge(s_sc.h, sc_g)
                        a_, b_, c_ = Ra[:, c, sl], Rb[:, c, sl], Rc[:, c, sl]
                        v.tensor_scalar(out=scr[:, 512:1024], in0=scr[:, 512:1024], scalar1=1.0, scalar2=None, op0=OP.max)
                        v.reciprocal(out=scr[:, 1024:1536], in_=scr[:, 512:1024])
                        v.tensor_tensor(out=a_, in0=a_, in1=scr[:, 1024:1536], op=OP.mult)
                        v.tensor_tensor(out=b_, in0=b_, in1=scr[:, 1024:1536], op=OP.mult)
                        r2 = v.tensor_tensor(out=c_, in0=c_, in1=scr[:, 1024:1536], op=OP.mult)
                        r2.then_inc(s_ve.h, 1)
                    block.vector(_)
                    s_ve.inc(1)

            for pp in range(4):
                pi = 12 + pp
                wa_gate = wa_base + 16 * (pi + 1)
                for blk in range(2):
                    nblk = pp * 2 + blk
                    for g in range(NG):
                        pband = P[4 + g]
                        ve_g = s_ve.n
                        sc_g = s_sc.n
                        def _(t, pi=pi, blk=blk, g=g, wa_gate=wa_gate, pband=pband, ve_g=ve_g, sc_g=sc_g):
                            t.wait_ge(s_wa.h, wa_gate)
                            t.wait_ge(s_ve.h, ve_g)
                            t.wait_ge(s_sc.h, sc_g)
                            slot = pi % 2
                            for c in range(8):
                                r = t.matmul(pband[:, :],
                                             ring[:, c, slot * 256 + blk * 128: slot * 256 + (blk + 1) * 128],
                                             xn_sb[:, c, g * 512:(g + 1) * 512],
                                             start=(c == 0), stop=(c == 7))
                            r.then_inc(s_mm.h, 1)
                        block.tensor(_)
                        s_mm.inc(1)
                        mm_g = s_mm.n
                        def _(v, nblk=nblk, g=g, pband=pband, mm_g=mm_g):
                            v.wait_ge(s_mm.h, mm_g)
                            r = v.tensor_tensor(out=Rb[:, nblk, g * 512:(g + 1) * 512],
                                                in0=Rb[:, nblk, g * 512:(g + 1) * 512],
                                                in1=pband[:, :], op=OP.mult)
                            r.then_inc(s_ve.h, 1)
                        block.vector(_)
                        s_ve.inc(1)
                if pi_loaded < 20:
                    emit_panel_load(pi_loaded, s_mm.n, None)
                    pi_loaded += 1

            for c in range(8):
                def _(v, c=c):
                    v.tensor_tensor_scan(out=scr[:, 0:TT], data0=Ra[:, c, :], data1=Rb[:, c, :],
                                         initial=carry[:, c:c + 1], op0=OP.mult, op1=OP.add)
                    v.tensor_tensor(out=Ra[:, c, 0:1], in0=Rc[:, c, 0:1], in1=carry[:, c:c + 1], op=OP.mult)
                    v.tensor_tensor(out=Ra[:, c, 1:], in0=Rc[:, c, 1:], in1=scr[:, 0:TT - 1], op=OP.mult)
                    r = v.tensor_copy(out=carry[:, c:c + 1], in_=scr[:, TT - 1:TT])
                    r.then_inc(s_ve.h, 1)
                block.vector(_)
                s_ve.inc(1)
            yhat_ready = s_ve.n

            for pp in range(4):
                pi = 16 + pp
                wa_gate = wa_base + 16 * (pi + 1)
                for blk in range(2):
                    oblk = pp * 2 + blk
                    for g in range(NG):
                        pband = P[4 + g]
                        ve_g = s_ve.n
                        def _(t, pi=pi, blk=blk, g=g, wa_gate=wa_gate, pband=pband, ve_g=ve_g):
                            t.wait_ge(s_wa.h, wa_gate)
                            t.wait_ge(s_ve.h, ve_g)
                            slot = pi % 2
                            for c in range(8):
                                r = t.matmul(pband[:, :],
                                             ring[:, c, slot * 256 + blk * 128: slot * 256 + (blk + 1) * 128],
                                             Ra[:, c, g * 512:(g + 1) * 512],
                                             start=(c == 0), stop=(c == 7))
                            r.then_inc(s_mm.h, 1)
                        block.tensor(_)
                        s_mm.inc(1)
                        mm_g = s_mm.n
                        def _(v, oblk=oblk, g=g, pband=pband, mm_g=mm_g):
                            v.wait_ge(s_mm.h, mm_g)
                            r = v.scalar_tensor_tensor(out=x_sb[:, oblk, g * 512:(g + 1) * 512],
                                                       in0=pband[:, :], scalar=attn_scale[:, :],
                                                       in1=x_sb[:, oblk, g * 512:(g + 1) * 512],
                                                       op0=OP.mult, op1=OP.add)
                            r.then_inc(s_ve.h, 1)
                        block.vector(_)
                        s_ve.inc(1)
                if pi_loaded < 20:
                    emit_panel_load(pi_loaded, s_mm.n, None)
                    pi_loaded += 1
            attn_done_mm = s_mm.n
            attn_done_ve = s_ve.n

            slot_gate_mm = [attn_done_mm, attn_done_mm]

            def emit_expert_load(e):
                slot = e % 2
                gate = slot_gate_mm[slot]
                def _(s, e=e, slot=slot, gate=gate):
                    s.wait_ge(s_mm.h, gate)
                    s.wait_ge(s_ve.h, attn_done_ve)
                    if e < NE:
                        s.dma_start(out=w13_slot[slot][:, :, :],
                                    in_=w13T[e].rearrange("(c p) f -> p c f", p=128)).then_inc(s_we[slot].h, 16)
                        s.dma_start(out=w2_slot(slot)[:, :, :],
                                    in_=w2T[e].rearrange("(c p) d2 -> p c d2", p=128)).then_inc(s_we[slot].h, 16)
                    else:
                        s.dma_start(out=w13_slot[slot][:, :, :],
                                    in_=w13sT.rearrange("(c p) f -> p c f", p=128)).then_inc(s_we[slot].h, 16)
                        s.dma_start(out=w2_slot(slot)[:, :, :],
                                    in_=w2sT.rearrange("(c p) d2 -> p c d2", p=128)).then_inc(s_we[slot].h, 16)
                s_we[slot].inc(32)
                block.sync(_)

            emit_expert_load(0)
            emit_expert_load(1)

            for e in range(NE + 1):
                slot = e % 2
                is_shared = (e == NE)
                w13s_ = w13_slot[slot]
                w2s_ = w2_slot(slot)
                we_gate = s_we[slot].n

                if not is_shared:
                    ve_g = s_ve.n
                    sc_g = s_sc.n
                    def _(t, e=e, ve_g=ve_g, sc_g=sc_g):
                        t.wait_ge(s_sc.h, comb_ready_sc)
                        t.wait_ge(s_sc.h, sc_g)
                        t.wait_ge(s_ve.h, ve_g)
                        t.matmul(P[6][:, :], sel_sb[:, e, :], comb_em[:, 0:512], start=True, stop=True)
                        r = t.matmul(P[7][:, :], sel_sb[:, e, :], comb_em[:, 512:1024], start=True, stop=True)
                        r.then_inc(s_mm.h, 1)
                    block.tensor(_)
                    s_mm.inc(1)
                    mm_g = s_mm.n
                    def _(s, mm_g=mm_g):
                        s.wait_ge(s_mm.h, mm_g)
                        s.copy(out=scr[:, 3072:3584], in_=P[6][:, :])
                        r = s.copy(out=scr[:, 3584:4096], in_=P[7][:, :])
                        r.then_inc(s_sc.h, 1)
                    block.scalar(_)
                    s_sc.inc(1)
                cb_ready = s_sc.n

                for g in range(NG):
                    ve_g = s_ve.n
                    def _(t, g=g, we_gate=we_gate, cb_ready=cb_ready, ve_g=ve_g, w13s_=w13s_, slot=slot):
                        t.wait_ge(s_we[slot].h, we_gate)
                        t.wait_ge(s_sc.h, cb_ready)
                        t.wait_ge(s_ve.h, ve_g)
                        for fc in range(4):
                            for c in range(8):
                                r = t.matmul(P[fc][:, :], w13s_[:, c, fc * 128:(fc + 1) * 128],
                                             xn_sb[:, c, g * 512:(g + 1) * 512],
                                             start=(c == 0), stop=(c == 7))
                            r.then_inc(s_mm.h, 1)
                    block.tensor(_)
                    s_mm.inc(4)
                    mm_gate_half = s_mm.n
                    def _(s, g=g, mm_gate_half=mm_gate_half):
                        for fc in range(4):
                            s.wait_ge(s_mm.h, mm_gate_half - 3 + fc)
                            r = s.activation(out=ring[:, 2 * fc + g, :], in_=P[fc][:, :], func=AF.Silu)
                        r.then_inc(s_sc.h, 1)
                    block.scalar(_)
                    s_sc.inc(1)
                    def _(t, g=g, w13s_=w13s_):
                        for fc in range(4):
                            for c in range(8):
                                r = t.matmul(P[4 + fc][:, :], w13s_[:, c, 512 + fc * 128: 512 + (fc + 1) * 128],
                                             xn_sb[:, c, g * 512:(g + 1) * 512],
                                             start=(c == 0), stop=(c == 7))
                            r.then_inc(s_mm.h, 1)
                    block.tensor(_)
                    s_mm.inc(4)
                    mm_up = s_mm.n
                    sc_silu = s_sc.n
                    def _(v, g=g, mm_up=mm_up, is_shared=is_shared, sc_silu=sc_silu):
                        v.wait_ge(s_sc.h, sc_silu)
                        for fc in range(4):
                            v.wait_ge(s_mm.h, mm_up - 3 + fc)
                            v.tensor_tensor(out=ring[:, 2 * fc + g, :],
                                            in0=ring[:, 2 * fc + g, :],
                                            in1=P[4 + fc][:, :], op=OP.mult)
                            if not is_shared:
                                v.tensor_tensor(out=ring[:, 2 * fc + g, :],
                                                in0=ring[:, 2 * fc + g, :],
                                                in1=scr[:, (6 + g) * 512:(7 + g) * 512], op=OP.mult)
                        r = v.engine_nop()
                        r.then_inc(s_ve.h, 1)
                    block.vector(_)
                    s_ve.inc(1)
                    h_ready = s_ve.n
                    def _(t, g=g, h_ready=h_ready, w2s_=w2s_):
                        t.wait_ge(s_ve.h, h_ready)
                        for db in range(8):
                            for fc in range(4):
                                r = t.matmul(P[db][:, :], w2s_[:, fc, db * 128:(db + 1) * 128],
                                             ring[:, 2 * fc + g, :],
                                             start=(fc == 0), stop=(fc == 3))
                            r.then_inc(s_mm.h, 1)
                    block.tensor(_)
                    s_mm.inc(8)
                    mm_y = s_mm.n
                    def _(v, g=g, mm_y=mm_y):
                        for db in range(8):
                            v.wait_ge(s_mm.h, mm_y - 7 + db)
                            r = v.tensor_tensor(out=x_sb[:, db, g * 512:(g + 1) * 512],
                                                in0=x_sb[:, db, g * 512:(g + 1) * 512],
                                                in1=P[db][:, :], op=OP.add)
                        r.then_inc(s_ve.h, 1)
                    block.vector(_)
                    s_ve.inc(1)

                slot_gate_mm[slot] = s_mm.n
                if e + 2 <= NE:
                    emit_expert_load(e + 2)

            x_done = s_ve.n
            def _(g, t0=t0, t1=t1, x_done=x_done):
                g.wait_ge(s_ve.h, x_done)
                g.dma_start(out=outT[:, t0:t1].rearrange("(c p) t -> p c t", p=128),
                            in_=x_sb[:, :, :]).then_inc(s_xout.h, 16)
            block.gpsimd(_)
            s_xout.inc(16)

        xo_final = s_xout.n
        def _(g, xo_final=xo_final):
            g.wait_ge(s_xout.h, xo_final)
        block.gpsimd(_)

    es0.close()
    return nc


_NC_CACHE = None


def kernel(L0_attn_u, L0_attn_W_in, L0_attn_W_out, L0_attn_Wp, L0_attn_bp,
           L0_attn_alpha_log, L0_attn_delta_bias, L0_attn_log_gamma,
           L0_ffn_input_ids, L0_ffn_gate_w, L0_ffn_gate_b, L0_ffn_w13, L0_ffn_w2,
           L0_ffn_w13_s, L0_ffn_w2_s):
    global _NC_CACHE
    u = np.asarray(L0_attn_u, dtype=np.float32)

    c = np.ascontiguousarray
    w_inT = c(np.asarray(L0_attn_W_in, np.float32).T)
    w_outT = c(np.asarray(L0_attn_W_out, np.float32).T)
    wpT = c(np.asarray(L0_attn_Wp, np.float32).T)
    gate_wT = c(np.asarray(L0_ffn_gate_w, np.float32).T)
    gate_b = c(np.asarray(L0_ffn_gate_b, np.float32).reshape(NE, 1))
    w13T = c(np.asarray(L0_ffn_w13, np.float32).transpose(0, 2, 1))
    w2T = c(np.asarray(L0_ffn_w2, np.float32).transpose(0, 2, 1))
    w13sT = c(np.asarray(L0_ffn_w13_s, np.float32).T)
    w2sT = c(np.asarray(L0_ffn_w2_s, np.float32).T)
    alpha_log = c(np.asarray(L0_attn_alpha_log, np.float32).reshape(8, 128).T)
    delta_bias = c(np.asarray(L0_attn_delta_bias, np.float32).reshape(8, 128).T)
    bp = c(np.asarray(L0_attn_bp, np.float32).reshape(3, 8, 128).transpose(2, 0, 1).reshape(128, 24))
    log_gamma = c(np.asarray(L0_attn_log_gamma, np.float32).reshape(1, 1))
    eye = np.eye(128, dtype=np.float32)
    sel = np.zeros((NE, NE, 128), dtype=np.float32)
    for e in range(NE):
        sel[e, e, :] = 1.0

    shared = dict(w_inT=w_inT, w_outT=w_outT, wpT=wpT, gate_wT=gate_wT, gate_b=gate_b,
                  w13T=w13T, w2T=w2T, w13sT=w13sT, w2sT=w2sT, alpha_log=alpha_log,
                  delta_bias=delta_bias, bp=bp, log_gamma=log_gamma, eye=eye, sel_e=sel)

    in_maps = []
    for b in range(N_CORES):
        m = dict(shared)
        m["xT"] = c(u[b].T)
        in_maps.append(m)

    if _NC_CACHE is None:
        _NC_CACHE = build_program()
    nc = _NC_CACHE

    res = run_bass_kernel_spmd(nc, in_maps, core_ids=list(range(N_CORES)))
    out = np.stack([np.asarray(res.results[b]["outT"]).T for b in range(N_CORES)], axis=0)
    return out
